# version 26
# baseline (speedup 1.0000x reference)
import os
import numpy as np
import ml_dtypes
import contextlib
import concourse.bass as bass
import concourse.mybir as mybir
from concourse.bass_utils import run_bass_kernel_spmd
from concourse.alu_op_type import AluOpType as Op

F32 = mybir.dt.float32
BF16 = mybir.dt.bfloat16
FP8 = mybir.dt.float8e4
AFT = mybir.ActivationFunctionType
AX = mybir.AxisListType
bfd = ml_dtypes.bfloat16

D_, H_, DL = 2048, 16, 512
SS = 1.0 / float(np.sqrt(128.0))
NEG = -1.0e9
ENGINES = ("sync", "gpsimd", "vector", "scalar", "tensor")


class KB:
    def __init__(self, nc):
        self.nc = nc
        self.ops = {e: [] for e in ENGINES}
        self.cnt = {}
        self.semh = {}
        self.lastwait = {}
        self.pt_ev = [0, 0]
        self.dma_pools = {}
        self.dma_rr = {}

    def init_pools(self, es, queues=("sync", "gpsimd"), n=6):
        for q in queues:
            self.dma_pools[q] = []
            for i in range(n):
                name = f"d{q[0]}{i}"
                self.sem(name, es)
                self.dma_pools[q].append(name)
            self.dma_rr[q] = 0

    def dma(self, queue, fn):
        name = self.dma_pools[queue][self.dma_rr[queue]]
        self.dma_rr[queue] = (self.dma_rr[queue] + 1) % len(self.dma_pools[queue])
        self.wait(queue, name, self.cnt[name])
        v = self.op(queue, fn, inc=name, n=16)
        return (name, v)

    def wait_all_dma(self, eng):
        for q, pool in self.dma_pools.items():
            for name in pool:
                self.wait(eng, name, self.cnt[name])

    def sem(self, name, es):
        self.semh[name] = es.enter_context(self.nc.semaphore(name))
        self.cnt[name] = 0

    def op(self, eng, fn, inc=None, n=1):
        if inc is None:
            self.ops[eng].append(fn)
            return None
        self.cnt[inc] += n
        v = self.cnt[inc]
        h = self.semh[inc]

        def w(e, fn=fn, h=h, n=n):
            fn(e).then_inc(h, n)

        self.ops[eng].append(w)
        return v

    def wait(self, eng, name, v):
        if v is None or v <= 0:
            return
        key = (eng, name)
        if self.lastwait.get(key, 0) >= v:
            return
        self.lastwait[key] = v
        h = self.semh[name]
        self.ops[eng].append(lambda e, h=h, v=v: e.wait_ge(h, v))

    def barrier(self):
        snap = dict(self.cnt)
        for eng in ENGINES:
            for s, v in snap.items():
                self.wait(eng, s, v)


class _Cut(Exception):
    pass


def build_nc():
    PH = float(os.environ.get("KPH", "99"))

    def cut(n):
        if PH <= n:
            raise _Cut()

    nc = bass.Bass("TRN2", target_bir_lowering=False)

    def din(name, shape, dt=F32):
        return nc.dram_tensor(name, shape, dt, kind="ExternalInput")

    xb = din("xb", [2048, 2048])
    wqT = din("wqT", [2048, 2048])
    wkvaT = din("wkvaT", [2048, 512])
    wkvbT = din("wkvbT", [512, 4096])
    outwT = din("outwT", [2048, 2048])
    gatew = din("gatew", [2048, 6144])
    upw = din("upw", [2048, 6144])
    downw = din("downw", [6144, 2048])
    wqb_c = din("wqb_c", [128, 16])
    outb_c = din("outb_c", [128, 16])
    normw_c = din("normw_c", [128, 4])
    gsc_c = din("gsc_c", [128, 48])
    usc_c = din("usc_c", [128, 48])
    dsc_c = din("dsc_c", [128, 16])
    c2_d = din("c2", [64, 16])
    m2_d = din("m2", [64, 128])
    identb_d = din("identb", [128, 128], BF16)
    identf_d = din("identf", [128, 128], F32)
    onesc_d = din("onesc", [128, 1], BF16)
    onesr_d = din("onesr", [1, 128], F32)
    masks_d = din("masks", [128, 2048], BF16)
    mbias_d = din("mbias", [128, 32], F32)
    out_d = nc.dram_tensor("out", [1024, 2048], F32, kind="ExternalOutput")
    res_d = nc.dram_tensor("res_scr", [1024, 2048], F32, kind="Internal")
    lrow_d = nc.dram_tensor("lrow_scr", [32, 512], F32, kind="Internal")
    linv_d = nc.dram_tensor("linv_scr", [32, 512], F32, kind="Internal")

    es = contextlib.ExitStack()
    sb = lambda n, sh, dt: es.enter_context(nc.sbuf_tensor(n, sh, dt))
    pt = lambda n, sh, dt: es.enter_context(nc.psum_tensor(n, sh, dt))

    BIG0 = sb("BIG0", [128, 32768], BF16)
    BIG1 = sb("BIG1", [128, 16384], BF16)
    BIG2 = sb("BIG2", [128, 16384], BF16)
    LATT = sb("LATT", [128, 8192], BF16)
    XHB = sb("XHB", [128, 4096], BF16)
    WT = sb("WT", [128, 4096], BF16)
    PB = sb("PB", [128, 1024], BF16)
    SQB = sb("SQB", [128, 2048], BF16)
    F32A = sb("F32A", [128, 4096], F32)
    W8 = sb("W8", [128, 8192], FP8)
    MASKS = sb("MASKS", [128, 2048], BF16)
    WQBc = sb("WQBc", [128, 16], F32)
    OUTBc = sb("OUTBc", [128, 16], F32)
    NORMWc = sb("NORMWc", [128, 4], F32)
    GSCc = sb("GSCc", [128, 48], F32)
    USCc = sb("USCc", [128, 48], F32)
    DSCc = sb("DSCc", [128, 16], F32)
    C2s = sb("C2s", [64, 16], F32)
    M2s = sb("M2s", [64, 128], F32)
    IDB = sb("IDB", [128, 128], BF16)
    IDF = sb("IDF", [128, 128], F32)
    ONESCb = sb("ONESCb", [128, 1], BF16)
    ONESRf = sb("ONESRf", [1, 128], F32)
    MBIASs = sb("MBIASs", [128, 32], F32)
    WROPE = sb("WROPE", [128, 16], F32)
    WQBG = sb("WQBG", [128, 16], F32)
    RSTD = sb("RSTD", [128, 16], F32)
    SCL = sb("SCL", [128, 16], F32)
    SCLSQ = sb("SCLSQ", [128, 16], F32)
    BIASC = sb("BIASC", [128, 32], F32)
    BNS = sb("BNS", [128, 24], F32)
    AGG = sb("AGG", [128, 16], F32)
    STAT = sb("STAT", [128, 8], F32)
    KC = sb("KC", [128, 16], F32)
    MAXQ = sb("MAXQ", [128, 1], F32)
    MAXK = sb("MAXK", [128, 1], F32)
    ROWA = sb("ROWA", [1, 128], F32)
    MUx = sb("MUx", [1, 4], F32)
    MUCOL = sb("MUCOL", [128, 1], F32)
    LINV = sb("LINV", [1, 512], F32)
    EPSC = sb("EPSC", [128, 2], F32)
    SEL = sb("SEL", [32, 384], F32)
    X4 = sb("X4", [128, 2048], BF16)

    PS = [pt(f"PS{i}", [128, 512], F32) for i in range(6)]
    PT = [pt("PT0", [128, 1024], BF16), pt("PT1", [128, 1024], BF16)]

    kb = KB(nc)
    for s in ("sT", "sV", "sS", "sG"):
        kb.sem(s, es)
    kb.init_pools(es)
    A = bass.AP

    def tp_group(src, n, pti, dst, src_f32=False):
        kb.wait("tensor", "sV", kb.pt_ev[pti])
        for j in range(n):
            tct = kb.op("tensor", lambda e, j=j: e.matmul(
                PT[pti][:, j * 128:(j + 1) * 128], src[:, j * 128:(j + 1) * 128], IDB[:, :],
                is_transpose=True, start=(j == 0), stop=(j == n - 1)),
                inc="sT" if j == n - 1 else None)
        kb.wait("vector", "sT", tct)
        out_ap, in_ap = dst(n)
        ev = kb.op("vector", lambda e, o=out_ap, i=in_ap: e.tensor_copy(o, i), inc="sV")
        kb.pt_ev[pti] = ev
        return ev

    def strided_dst(base, off, n, chunk_stride, width_elems):
        out_ap = A(base, off, [[width_elems, 128], [chunk_stride, n], [1, 128]])
        return out_ap

    def load_strip(src_ap, buf, n_t, wait_mm=None):
        if wait_mm:
            kb.wait("gpsimd", "sT", wait_mm)
        return kb.dma("gpsimd", lambda e, s=src_ap, buf=buf, n_t=n_t: e.dma_start(
            out=WT[:, buf * 2048:buf * 2048 + n_t * 128], in_=s))

    try:
        XT = [F32A[:, 0:2048], F32A[:, 2048:4096]]
        consts = [(IDB, identb_d), (M2s, m2_d), (C2s, c2_d), (WQBc, wqb_c),
                  (IDF, identf_d), (ONESCb, onesc_d), (ONESRf, onesr_d),
                  (MASKS, masks_d), (MBIASs, mbias_d),
                  (OUTBc, outb_c), (NORMWc, normw_c), (GSCc, gsc_c),
                  (USCc, usc_c), (DSCc, dsc_c)]
        cld = {}
        for dst0, src0 in consts[:4]:
            cld[dst0.name] = kb.dma("sync", lambda e, d=dst0, s=src0: e.dma_start(
                out=d[:, :], in_=s.ap()[:, :]))
        preA = {it: kb.dma("sync" if it == 0 else "gpsimd", lambda e, it=it: e.dma_start(
            out=XT[it % 2], in_=xb.ap()[it * 128:(it + 1) * 128, :])) for it in (0, 1)}
        for dst0, src0 in consts[4:]:
            cld[dst0.name] = kb.dma("sync", lambda e, d=dst0, s=src0: e.dma_start(
                out=d[:, :], in_=s.ap()[:, :]))
        kb.op("vector", lambda e: e.memset(EPSC[:, 0:1], 1.0e-5))
        kb.op("vector", lambda e: e.memset(EPSC[:, 1:2], 1.0e-6))
        kb.op("gpsimd", lambda e: e.iota(
            SEL[0:32, 256:384], [[0, 128]], base=0, channel_multiplier=1,
            allow_small_or_imprecise_dtypes=True), inc="sG")
        kb.wait("vector", *cld["C2s"])
        kb.wait("vector", *cld["WQBc"])
        v_sq = kb.op("vector", lambda e: e.tensor_mul(AGG[0:64, 0:16], C2s[0:64, :], C2s[0:64, :]), inc="sV")
        kb.wait("tensor", "sV", v_sq)
        kb.wait("tensor", *cld["M2s"])
        kb.wait("tensor", *cld["IDB"])
        t_g = kb.op("tensor", lambda e: e.matmul(PS[5][:, 0:16], M2s[0:64, :], AGG[0:64, 0:16],
                                                 start=True, stop=True), inc="sT")
        kb.op("vector", lambda e: e.memset(WROPE[0:64, 0:16], 1.0))
        kb.wait("vector", "sT", t_g)
        vw = kb.op("vector", lambda e: e.tensor_copy(WROPE[64:128, 0:16], PS[5][64:128, 0:16]), inc="sV")
        kb.wait("vector", "sV", vw)
        kb.op("vector", lambda e: e.tensor_mul(WQBG[:, 0:16], WROPE[:, 0:16], WQBc[:, 0:16]))

        cut(1)
        strips_B = {dl: load_strip(A(wkvaT, dl * 128, [[512, 128], [128 * 512, 16], [1, 128]]),
                                   dl % 2, 16) for dl in (0, 1)}
        def vedge(fn):
            v = kb.op("vector", fn, inc="sV")
            kb.wait("vector", "sV", v)
            return v

        def warmT(upto, cap=5):
            lo = max(kb.lastwait.get(("tensor", "sV"), 0) + 1, upto - cap)
            for v in range(lo, upto):
                kb.wait("tensor", "sV", v)
                kb.op("tensor", lambda e: e.matmul(PS[5][0:1, 0:1], IDB[:, 0:1], IDB[:, 0:1],
                                                   start=True, stop=True))

        def ln_tile(buf, xh, eps_col):
            kb.op("scalar", lambda e, buf=buf: e.activation(
                BIG2[:, 0:2048], buf, AFT.Identity, accum_out=STAT[:, 0:1]), inc="sS")
            s_sq = kb.op("scalar", lambda e, buf=buf: e.activation(
                SQB[:, 0:2048], buf, AFT.Square, accum_out=STAT[:, 1:2]), inc="sS")
            kb.wait("vector", "sS", s_sq)
            vedge(lambda e: e.tensor_scalar(STAT[:, 2:3], STAT[:, 0:1], 1.0 / 2048, None, Op.mult))
            vedge(lambda e: e.tensor_mul(STAT[:, 3:4], STAT[:, 2:3], STAT[:, 2:3]))
            v_var = kb.op("vector", lambda e: e.scalar_tensor_tensor(
                STAT[:, 5:6], STAT[:, 1:2], 1.0 / 2048, STAT[:, 3:4],
                Op.mult, Op.subtract), inc="sV")
            kb.wait("vector", "sV", v_var)
            kb.wait("scalar", "sV", v_var)
            s_sd = kb.op("scalar", lambda e, eps_col=eps_col: e.activation(
                STAT[:, 6:7], STAT[:, 5:6], AFT.Sqrt, bias=EPSC[:, eps_col:eps_col + 1],
                scale=1.0), inc="sS")
            kb.wait("vector", "sS", s_sd)
            vedge(lambda e: e.reciprocal(STAT[:, 7:8], STAT[:, 6:7]))
            return kb.op("vector", lambda e, xh=xh, buf=buf: e.tensor_scalar(
                xh, buf, STAT[:, 2:3], STAT[:, 7:8], Op.subtract, Op.mult), inc="sV")

        lnct, evA = {}, {}
        for it in range(16):
            if it < 2:
                ld = preA[it]
            else:
                qx = "sync" if it % 2 == 0 else "gpsimd"
                kb.wait(qx, "sV", lnct[it - 2])
                ld = kb.dma(qx, lambda e, it=it: e.dma_start(
                    out=XT[it % 2], in_=xb.ap()[it * 128:(it + 1) * 128, :]))
            kb.wait("vector", *ld)
            kb.wait("scalar", *ld)
            xh = XHB[:, (it % 2) * 2048:(it % 2) * 2048 + 2048]
            if it >= 2:
                kb.wait("vector", "sT", evA[it - 2])
            lnct[it] = ln_tile(XT[it % 2], xh, 0)
            warmT(lnct[it])
            kb.wait("tensor", "sV", lnct[it])
            for half in range(2):
                src = xh[:, half * 1024:half * 1024 + 1024]
                dst = lambda n, half=half, it=it, pti=half: (
                    A(BIG0, half * 8 * 2048 + it * 128, [[32768, 128], [2048, 8], [1, 128]]),
                    PT[half][:, 0:1024])
                evA[it] = tp_group(src, 8, half, dst)
            evA[it] = kb.cnt["sT"]
        kb.barrier()

        cut(2)
        first_col = [True]
        lat_ev = {}
        mm_lat = {}
        col_prev = None
        pending = None

        def emit_cols(idx, tb, sqct, is_last):
            kb.wait("tensor", "sS", sqct)
            slot = idx % 2
            cp = None
            for tt in range(4):
                isf = first_col[0]
                first_col[0] = False
                isl = (is_last and tt == 3)
                cp = kb.op("tensor", lambda e, tt=tt, tb=tb, slot=slot, isf=isf, isl=isl: e.matmul(
                    PS[4][:, tb * 4 + tt:tb * 4 + tt + 1],
                    SQB[:, slot * 512 + tt * 128:slot * 512 + (tt + 1) * 128], ONESCb[:, 0:1],
                    start=isf, stop=isl), inc="sT")
            return cp

        for dl in range(4):
            if dl < 2:
                strip = strips_B[dl]
            else:
                strip = load_strip(A(wkvaT, dl * 128, [[512, 128], [128 * 512, 16], [1, 128]]),
                                   dl % 2, 16, wait_mm=mm_lat.get(dl - 2))
            if dl == 0:
                cut(2.2)
            kb.wait("tensor", *strip)
            for tb in range(4):
                idx = dl * 4 + tb
                pi = idx % 4
                prev = lat_ev.get(idx - 4)
                if prev:
                    kb.wait("tensor", "sS", prev)
                for c in range(16):
                    mmct = kb.op("tensor", lambda e, c=c, pi=pi, dl=dl, tb=tb: e.matmul(
                        PS[pi][:, :], WT[:, (dl % 2) * 2048 + c * 128:(dl % 2) * 2048 + (c + 1) * 128],
                        BIG0[:, c * 2048 + tb * 512:c * 2048 + tb * 512 + 512],
                        start=(c == 0), stop=(c == 15)), inc="sT" if c == 15 else None)
                mm_lat[dl] = mmct
                if dl == 0 and tb == 0:
                    cut(2.3)
                kb.wait("scalar", "sT", mmct)
                if col_prev:
                    kb.wait("scalar", "sT", col_prev)
                v_sq2 = kb.op("scalar", lambda e, pi=pi, idx=idx: e.activation(
                    SQB[:, (idx % 2) * 512:(idx % 2) * 512 + 512], PS[pi][:, :],
                    AFT.Square), inc="sS")
                if dl == 0 and tb == 0:
                    cut(2.45)
                lat_ev[idx] = kb.op("scalar", lambda e, pi=pi, dl=dl, tb=tb: e.activation(
                    LATT[:, dl * 2048 + tb * 512:dl * 2048 + tb * 512 + 512],
                    PS[pi][:, :], AFT.Identity, scale=NORMWc[:, dl:dl + 1]), inc="sS")
                if dl == 0 and tb == 0:
                    cut(2.6)
                if pending is not None:
                    col_prev = emit_cols(*pending, is_last=False)
                pending = (idx, tb, v_sq2)
        strips_C = {ot: load_strip(A(wqT, ot * 128, [[2048, 128], [128 * 2048, 16], [1, 128]]),
                                   ot % 2, 16, wait_mm=mm_lat[2 + ot]) for ot in (0, 1)}
        col_prev = emit_cols(*pending, is_last=True)
        kb.wait("scalar", "sT", col_prev)
        s_sr = kb.op("scalar", lambda e: e.activation(RSTD[:, 0:16], PS[4][:, 0:16], AFT.Sqrt,
                                                      bias=EPSC[:, 1:2], scale=1.0 / 512), inc="sS")
        kb.wait("vector", "sS", s_sr)
        vedge(lambda e: e.reciprocal(RSTD[:, 0:16], RSTD[:, 0:16]))
        vedge(lambda e: e.tensor_scalar(SCL[:, 0:16], RSTD[:, 0:16], SS, None, Op.mult))
        vedge(lambda e: e.tensor_mul(SCLSQ[:, 0:16], SCL[:, 0:16], SCL[:, 0:16]))
        kb.barrier()

        cut(3)
        q_ev = {}
        mm_q = {}
        qcol_by = {}
        qn_red = {}
        v_mq_box = [None]

        def emit_qnorm(h):
            sqb_h = SQB if h % 2 == 0 else X4
            kb.wait("vector", "sS", q_ev[h * 2 + 1])
            if h >= 2:
                kb.wait("vector", "sT", qcol_by[h - 2])
            v_qsq = kb.op("vector", lambda e, h=h, sqb_h=sqb_h: e.tensor_mul(
                sqb_h[:, 0:1024], BIG1[:, h * 1024:(h + 1) * 1024],
                BIG1[:, h * 1024:(h + 1) * 1024]), inc="sV")
            kb.wait("tensor", "sV", v_qsq)
            if h >= 1:
                kb.wait("tensor", "sV", qn_red[h - 1])
            for tq in range(8):
                qcol = kb.op("tensor", lambda e, tq=tq, sqb_h=sqb_h: e.matmul(
                    PS[4][:, tq:tq + 1], sqb_h[:, tq * 128:(tq + 1) * 128], ONESCb[:, 0:1],
                    start=(tq == 0), stop=(tq == 7)), inc="sT" if tq == 7 else None)
            qcol_by[h] = qcol
            kb.wait("vector", "sT", qcol)
            qn_red[h] = vedge(lambda e: e.tensor_reduce(STAT[:, 0:1], PS[4][:, 0:8], AX.X, Op.max))
            if h == 0:
                v_mq_box[0] = vedge(lambda e: e.tensor_copy(MAXQ[:, 0:1], STAT[:, 0:1]))
            else:
                v_mq_box[0] = vedge(lambda e: e.tensor_max(MAXQ[:, 0:1], MAXQ[:, 0:1], STAT[:, 0:1]))

        for ot in range(16):
            if ot < 2:
                strip = strips_C[ot]
            else:
                strip = load_strip(A(wqT, ot * 128, [[2048, 128], [128 * 2048, 16], [1, 128]]),
                                   ot % 2, 16, wait_mm=mm_q.get(ot - 2))
            kb.wait("tensor", *strip)
            for qb in range(2):
                pi = (ot * 2 + qb) % 4
                prev = q_ev.get(ot * 2 + qb - 4)
                if prev:
                    kb.wait("tensor", "sS", prev)
                for c in range(16):
                    mmct = kb.op("tensor", lambda e, c=c, pi=pi, ot=ot, qb=qb: e.matmul(
                        PS[pi][:, :], WT[:, (ot % 2) * 2048 + c * 128:(ot % 2) * 2048 + (c + 1) * 128],
                        BIG0[:, c * 2048 + qb * 512:c * 2048 + qb * 512 + 512],
                        start=(c == 0), stop=(c == 15)), inc="sT" if c == 15 else None)
                mm_q[ot] = mmct
                kb.wait("scalar", "sT", mmct)
                q_ev[ot * 2 + qb] = kb.op("scalar", lambda e, pi=pi, ot=ot, qb=qb: e.activation(
                    BIG1[:, ot * 1024 + qb * 512:ot * 1024 + qb * 512 + 512],
                    PS[pi][:, :], AFT.Identity, bias=WQBG[:, ot:ot + 1],
                    scale=WROPE[:, ot:ot + 1]), inc="sS")
            if ot >= 1:
                emit_qnorm(ot - 1)
        emit_qnorm(15)
        v_mq = v_mq_box[0]
        kb.wait("gpsimd", "sV", v_mq)
        g_mq = kb.dma("gpsimd", lambda e: e.dma_start(out=ROWA[0:1, 0:128], in_=MAXQ[0:128, 0:1],
                                                      allow_slow_non_contiguous=True))
        kb.wait("vector", *g_mq)
        kb.op("vector", lambda e: e.tensor_reduce(MUx[0:1, 0:1], ROWA[0:1, 0:128], AX.X, Op.max))
        kb.barrier()

        cut(4)
        KOFF, VOFF = 0, 16384
        slot_lists = {0: [0, 1, 2, 3, 8, 9, 10, 11, 12, 13, 14, 15], 1: list(range(16))}
        pair_idx = 0
        bcast_rd = {}
        wr_by = {}
        PSL = [PB[:, 0:512], PB[:, 512:1024], SQB[:, 0:512], SQB[:, 512:1024],
               X4[:, 0:512], X4[:, 512:1024], X4[:, 1024:1536], X4[:, 1536:2048]]
        PADD = [SQB[:, 1024:1536], SQB[:, 1536:2048]]
        linv_by_pair = {}
        otev_by_pair = {}
        exp_ring = {}
        av_ring = {}
        av_by = {}
        g_ctr = [0]
        bcast_S = [None]

        for hg in range(2):
            k_ev = {}
            mm_k = {}
            vksq_by = {}
            kcol_k = {}
            kn_red = {}
            v_mk_box = [None]
            pend_kn = [None]

            def emit_kn(i8):
                sqb_i = SQB if i8 % 2 == 0 else X4
                kb.wait("tensor", "sV", vksq_by[i8])
                if i8 >= 1:
                    kb.wait("tensor", "sV", kn_red[i8 - 1])
                for kt in range(16):
                    kcol = kb.op("tensor", lambda e, kt=kt, sqb_i=sqb_i: e.matmul(
                        PS[4][:, kt:kt + 1], sqb_i[:, kt * 128:(kt + 1) * 128], ONESCb[:, 0:1],
                        start=(kt == 0), stop=(kt == 15)), inc="sT" if kt == 15 else None)
                kcol_k[i8] = kcol
                kb.wait("vector", "sT", kcol)
                kn_red[i8] = vedge(lambda e: e.tensor_mul(KC[:, 0:16], PS[4][:, 0:16], SCLSQ[:, 0:16]))
                vedge(lambda e: e.tensor_reduce(STAT[:, 0:1], KC[:, 0:16], AX.X, Op.max))
                if i8 == 0:
                    v_mk_box[0] = vedge(lambda e: e.tensor_copy(MAXK[:, 0:1], STAT[:, 0:1]))
                else:
                    v_mk_box[0] = vedge(lambda e: e.tensor_max(MAXK[:, 0:1], MAXK[:, 0:1], STAT[:, 0:1]))

            for i8 in range(8):
                h = hg * 8 + i8
                strip = load_strip(A(wkvbT, h * 128, [[4096, 128], [128 * 4096, 4], [1, 128]]),
                                   i8 % 2, 4, wait_mm=mm_k.get(i8 - 2))
                kb.wait("tensor", *strip)
                prev = k_ev.get(i8 - 1)
                if prev:
                    kb.wait("tensor", "sV", prev)
                for tb in range(4):
                    for c in range(4):
                        mmct = kb.op("tensor", lambda e, c=c, tb=tb, i8=i8: e.matmul(
                            PS[tb][:, :], WT[:, (i8 % 2) * 2048 + c * 128:(i8 % 2) * 2048 + (c + 1) * 128],
                            LATT[:, c * 2048 + tb * 512:c * 2048 + tb * 512 + 512],
                            start=(c == 0), stop=(c == 3)),
                            inc="sT" if (tb == 3 and c == 3) else None)
                mm_k[i8] = mmct
                kb.wait("vector", "sT", mmct)
                for tb in range(4):
                    k_ev[i8] = kb.op("vector", lambda e, tb=tb, i8=i8: e.tensor_copy(
                        BIG0[:, KOFF + i8 * 2048 + tb * 512:KOFF + i8 * 2048 + tb * 512 + 512],
                        PS[tb][:, :]), inc="sV")
                kb.wait("vector", "sV", k_ev[i8])
                if i8 >= 2:
                    kb.wait("vector", "sT", kcol_k[i8 - 2])
                vksq_by[i8] = kb.op("vector", lambda e, i8=i8, sqb_i=(SQB if i8 % 2 == 0 else X4): e.tensor_mul(
                    sqb_i[:, 0:2048], BIG0[:, KOFF + i8 * 2048:KOFF + (i8 + 1) * 2048],
                    BIG0[:, KOFF + i8 * 2048:KOFF + (i8 + 1) * 2048]), inc="sV")
                if pend_kn[0] is not None:
                    emit_kn(pend_kn[0])
                pend_kn[0] = i8
            emit_kn(7)
            v_mk = v_mk_box[0]
            cut(5 - hg)
            vin = A(wkvbT, 2048 + hg * 1024, [[4096, 128], [128 * 4096, 4], [1, 1024]])
            wv_ld = kb.dma("gpsimd", lambda e, vin=vin: e.dma_start(
                out=A(XHB, 0, [[4096, 128], [1024, 4], [1, 1024]]), in_=vin))
            kb.wait("gpsimd", "sV", v_mk)
            g_mk = kb.dma("gpsimd", lambda e: e.dma_start(out=ROWA[0:1, 0:128], in_=MAXK[0:128, 0:1],
                                                          allow_slow_non_contiguous=True))
            kb.wait("vector", *g_mk)
            vedge(lambda e: e.tensor_reduce(MUx[0:1, 1:2], ROWA[0:1, 0:128], AX.X, Op.max))
            v_mu2 = vedge(lambda e: e.tensor_mul(MUx[0:1, 2:3], MUx[0:1, 0:1], MUx[0:1, 1:2]))
            kb.wait("scalar", "sV", v_mu2)
            s_mu = kb.op("scalar", lambda e: e.activation(MUx[0:1, 3:4], MUx[0:1, 2:3],
                                                          AFT.Sqrt), inc="sS")
            kb.wait("vector", "sS", s_mu)
            v_mun = kb.op("vector", lambda e: e.tensor_scalar(MUx[0:1, 3:4], MUx[0:1, 3:4],
                                                              -1.0, None, Op.mult), inc="sV")
            kb.wait("tensor", "sV", v_mun)
            t_muc = kb.op("tensor", lambda e: e.matmul(PS[5][:, 0:1], ONESRf[0:1, :], MUx[0:1, 3:4],
                                                       start=True, stop=True), inc="sT")
            kb.wait("vector", "sT", t_muc)
            vedge(lambda e: e.tensor_copy(MUCOL[:, 0:1], PS[5][:, 0:1]))
            v_bias = vedge(lambda e: e.tensor_scalar(BIASC[:, 0:32], MBIASs[:, 0:32],
                                                     MUCOL[:, 0:1], None, Op.add))
            cut(7 - hg)
            kb.wait("tensor", *wv_ld)
            kb.wait("tensor", "sV", k_ev[7])
            v_ev = {}
            vts_by = {}
            for tt in range(16):
                pb0 = (tt % 2) * 2
                if tt >= 2:
                    kb.wait("tensor", "sS", v_ev[(tt - 2, 1)])
                    kb.wait("tensor", "sV", vts_by[tt - 2])
                for dl in range(4):
                    for ovb in range(2):
                        mmv = kb.op("tensor", lambda e, tt=tt, dl=dl, ovb=ovb, pb0=pb0: e.matmul(
                            PS[pb0 + ovb][:, :], LATT[:, dl * 2048 + tt * 128:dl * 2048 + tt * 128 + 128],
                            XHB[:, dl * 1024 + ovb * 512:dl * 1024 + ovb * 512 + 512],
                            start=(dl == 0), stop=(dl == 3)),
                            inc="sT" if (dl == 3 and ovb == 1) else None)
                if tt == 0:
                    cut(5.6 - hg)
                kb.wait("vector", "sT", mmv)
                if tt >= 1:
                    kb.wait("vector", "sV", vts_by[tt - 1])
                vc = kb.op("vector", lambda e, pb0=pb0: e.tensor_copy(
                    F32A[:, 0:512], PS[pb0][:, :]), inc="sV")
                kb.wait("vector", "sV", vc)
                vts_by[tt] = kb.op("vector", lambda e, tt=tt: e.tensor_scalar(
                    BIG0[:, VOFF + tt * 1024:VOFF + tt * 1024 + 512],
                    F32A[:, 0:512], RSTD[:, tt:tt + 1], None, Op.mult), inc="sV")
                kb.wait("scalar", "sT", mmv)
                v_ev[(tt, 1)] = kb.op("scalar", lambda e, tt=tt, pb0=pb0: e.activation(
                    BIG0[:, VOFF + tt * 1024 + 512:VOFF + tt * 1024 + 1024],
                    PS[pb0 + 1][:, :], AFT.Identity, scale=RSTD[:, tt:tt + 1]), inc="sS")
            kb.barrier()
            cut(6 - hg)
            for i8 in range(8):
                h = hg * 8 + i8
                for qb in range(2):
                    slots = slot_lists[qb]
                    nsl = len(slots)
                    npair = nsl // 2
                    po = 2 + (pair_idx % 2)
                    gstart = g_ctr[0]
                    rdy = {}
                    padd_by = {}

                    def emit_pair(p, i8=i8, qb=qb, po=po, gstart=gstart, npair=None, nsl=None):
                        kb.wait("tensor", "sV", padd_by[p])
                        if p == 0 and pair_idx > 0:
                            kb.wait("tensor", "sV", linv_by_pair[pair_idx - 1])
                            if pair_idx > 1:
                                kb.wait("tensor", "sV", otev_by_pair[pair_idx - 2])
                        kb.op("tensor", lambda e, p=p, isf=(p == 0), isl=(p == npair - 1): e.matmul(
                            PS[5][0:1, 0:512], ONESCb[:, 0:1], PADD[p % 2],
                            start=isf, stop=isl))
                        for i in (2 * p, 2 * p + 1):
                            kt_i = slot_lists[qb][i]
                            pi_i = (gstart + i) % 8
                            av_by[(pair_idx, i)] = av_ring[gstart + i] = kb.op(
                                "tensor", lambda e, pi_i=pi_i, kt_i=kt_i, i8=i8,
                                isf=(i == 0), isl=(i == nsl - 1), po=po: e.matmul(
                                    PS[po][:, :],
                                    BIG0[:, VOFF + kt_i * 1024 + i8 * 128:VOFF + kt_i * 1024 + i8 * 128 + 128],
                                    PSL[pi_i],
                                    start=isf, stop=isl), inc="sT")

                    for j, kt in enumerate(slots):
                        g = gstart + j
                        pi = g % 8
                        sb_ = [PS[0], PS[1], PS[4]][g % 3]
                        if g >= 3:
                            kb.wait("tensor", "sS", exp_ring[g - 3])
                        t_s = kb.op("tensor", lambda e, sb_=sb_, i8=i8, kt=kt, qb=qb, h=h: e.matmul(
                            sb_[:, :],
                            BIG0[:, KOFF + i8 * 2048 + kt * 128:KOFF + i8 * 2048 + kt * 128 + 128],
                            BIG1[:, h * 1024 + qb * 512:h * 1024 + qb * 512 + 512],
                            start=True, stop=True), inc="sT")
                        kb.wait("scalar", "sT", t_s)
                        if g >= 8:
                            kb.wait("scalar", "sT", av_ring[g - 8])
                        exp_ring[g] = kb.op("scalar", lambda e, pi=pi, sb_=sb_, qb=qb, kt=kt: e.activation(
                            PSL[pi], sb_[:, :], AFT.Exp,
                            bias=BIASC[:, qb * 16 + kt:qb * 16 + kt + 1],
                            scale=SCL[:, kt:kt + 1]), inc="sS")
                        if 4 * qb <= kt <= 4 * qb + 3:
                            kb.wait("vector", "sS", exp_ring[g])
                            m = kt - 4 * qb
                            mk = kb.op("vector", lambda e, pi=pi, m=m: e.tensor_mul(
                                PSL[pi], PSL[pi], MASKS[:, m * 512:(m + 1) * 512]), inc="sV")
                            rdy[j] = ("sV", mk)
                        else:
                            rdy[j] = ("sS", exp_ring[g])
                        if j % 2 == 1:
                            p = j // 2
                            kb.wait("vector", *rdy[j - 1])
                            kb.wait("vector", *rdy[j])
                            if p >= 2:
                                kb.wait("vector", "sT", av_by[(pair_idx, 2 * (p - 2))])
                            padd_by[p] = kb.op("vector", lambda e, p=p, gstart=gstart: e.tensor_add(
                                PADD[p % 2], PSL[(gstart + 2 * p) % 8],
                                PSL[(gstart + 2 * p + 1) % 8]), inc="sV")
                            if p >= 1:
                                emit_pair(p - 1, npair=npair, nsl=nsl)
                    emit_pair(npair - 1, npair=npair, nsl=nsl)
                    g_ctr[0] += nsl
                    pgl = pair_idx % 16
                    kb.wait("vector", "sT", av_by[(pair_idx, nsl - 1)])
                    if (pair_idx - 1) in wr_by:
                        kb.wait("vector", *wr_by[pair_idx - 1])
                    linv_by_pair[pair_idx] = kb.op("vector", lambda e: e.tensor_copy(
                        LINV[0:1, 0:512], PS[5][0:1, 0:512]), inc="sV")
                    kb.wait("gpsimd", "sV", linv_by_pair[pair_idx])
                    wr_by[pair_idx] = kb.dma("gpsimd", lambda e, row=pair_idx: e.dma_start(
                        out=F32A[row:row + 1, 3072:3584], in_=LINV[0:1, 0:512],
                        allow_slow_non_contiguous=True))
                    otev_by_pair[pair_idx] = kb.op("vector", lambda e, po=po, h=h, qb=qb: e.tensor_copy(
                        BIG2[:, h * 1024 + qb * 512:h * 1024 + qb * 512 + 512],
                        PS[po][:, :]), inc="sV")
                    pair_idx += 1

            kb.barrier()

        cut(8)
        v_r32 = vedge(lambda e: e.reciprocal(
            F32A[0:32, 2560:3072], F32A[0:32, 3072:3584]))
        kb.wait("tensor", "sV", v_r32)
        last_n = None
        nbcs = {}
        mm_by = {}
        order = list(range(0, 32, 2)) + list(range(1, 32, 2))
        for k, p in enumerate(order):
            hq, qbp = p // 2, p % 2
            s = k % 2
            if k >= 2:
                kb.wait("vector", "sT", mm_by[order[k - 2]])
            v_sel = kb.op("vector", lambda e, p=p, s=s: e.tensor_scalar(
                SEL[0:32, s * 128:(s + 1) * 128], SEL[0:32, 256:384],
                float(p), None, Op.is_equal), inc="sV")
            kb.wait("tensor", "sV", v_sel)
            if k >= 2:
                kb.wait("tensor", "sV", nbcs[order[k - 2]])
            mm_by[p] = mmb = kb.op("tensor", lambda e, s=s: e.matmul(
                PS[2 + s][:, 0:512], SEL[0:32, s * 128:(s + 1) * 128],
                F32A[0:32, 2560:3072], start=True, stop=True), inc="sT")
            kb.wait("vector", "sT", mmb)
            last_n = kb.op("vector", lambda e, hq=hq, qbp=qbp, s=s: e.tensor_mul(
                BIG2[:, hq * 1024 + qbp * 512:hq * 1024 + qbp * 512 + 512],
                PS[2 + s][:, 0:512],
                BIG2[:, hq * 1024 + qbp * 512:hq * 1024 + qbp * 512 + 512]), inc="sV")
            nbcs[p] = last_n
        add_ct = {}
        mm_f = {}
        tctF = {}
        store_F = {}
        prev_store = [None]
        pend = [None]

        def flush_F(nxt_mm=None):
            pi, dt, qb, s_ev, ld_blk, xsl = pend[0]
            gi = dt * 2 + qb
            pst = PS[4 + gi % 2]
            aout = F32A[:, 3584:4096] if gi % 2 == 0 else F32A[:, 1024:1536]
            kb.wait("tensor", "sS", s_ev)
            pa = add_ct.get(gi - 2)
            if pa:
                kb.wait("tensor", "sV", pa[1])
            ate = F32A[:, 2048 + (gi % 2) * 512:2048 + (gi % 2) * 512 + 512]
            for tq in range(4):
                tct = kb.op("tensor", lambda e, tq=tq, ate=ate, pst=pst: e.matmul(
                    pst[:, tq * 128:(tq + 1) * 128], ate[:, tq * 128:(tq + 1) * 128], IDF[:, :],
                    is_transpose=True, start=(tq == 0), stop=(tq == 3)),
                    inc="sT" if tq == 3 else None)
            tctF[gi] = tct
            kb.wait("vector", "sT", tct)
            kb.wait("vector", *ld_blk)
            if (gi - 2) in store_F:
                kb.wait("vector", *store_F[gi - 2])
            v_add = kb.op("vector", lambda e, xsl=xsl, pst=pst, aout=aout: e.tensor_add(
                aout, pst[:, 0:512], F32A[:, xsl * 512:xsl * 512 + 512]), inc="sV")
            add_ct[gi] = (s_ev, v_add)
            kb.wait("sync", "sV", v_add)
            rout = A(res_d, (qb * 512) * 2048 + dt * 128, [[2048, 128], [128 * 2048, 4], [1, 128]])
            store_F[gi] = kb.dma("sync", lambda e, rout=rout, aout=aout: e.dma_start(
                out=rout, in_=aout))
            prev_store[0] = store_F[gi]
            pend[0] = None

        for dt in range(16):
            strip = load_strip(A(outwT, dt * 128, [[2048, 128], [128 * 2048, 16], [1, 128]]),
                               dt % 2, 16, wait_mm=mm_f.get(dt - 2))
            kb.wait("tensor", *strip)
            for qb in range(2):
                pi = (dt * 2 + qb) % 2
                prev = add_ct.get(dt * 2 + qb - 2)
                if prev:
                    kb.wait("tensor", "sS", prev[0])
                for c in range(16):
                    if dt == 0:
                        kb.wait("tensor", "sV", nbcs[c * 2 + qb])
                    mmct = kb.op("tensor", lambda e, c=c, pi=pi, dt=dt, qb=qb: e.matmul(
                        PS[pi][:, :], WT[:, (dt % 2) * 2048 + c * 128:(dt % 2) * 2048 + (c + 1) * 128],
                        BIG2[:, c * 1024 + qb * 512:c * 1024 + qb * 512 + 512],
                        start=(c == 0), stop=(c == 15)), inc="sT" if c == 15 else None)
                mm_f[dt] = mmct
                kb.wait("scalar", "sT", mmct)
                prev_tct = tctF.get(dt * 2 + qb - 1)
                if prev_tct:
                    kb.wait("scalar", "sT", prev_tct)
                ate = F32A[:, 2048 + pi * 512:2048 + pi * 512 + 512]
                s_ev = kb.op("scalar", lambda e, pi=pi, dt=dt, ate=ate: e.activation(
                    ate, PS[pi][:, :], AFT.Identity, bias=OUTBc[:, dt:dt + 1], scale=1.0), inc="sS")
                xin = A(xb, (qb * 512) * 2048 + dt * 128, [[2048, 128], [128 * 2048, 4], [1, 128]])
                xsl = (dt * 2 + qb) % 2
                prev_add = add_ct.get(dt * 2 + qb - 2)
                if prev_add:
                    kb.wait("sync", "sV", prev_add[1])
                ld_blk = kb.dma("sync", lambda e, xin=xin, xsl=xsl: e.dma_start(
                    out=F32A[:, xsl * 512:xsl * 512 + 512], in_=xin))
                if pend[0] is not None:
                    flush_F()
                pend[0] = (pi, dt, qb, s_ev, ld_blk, xsl)
        flush_F()
        kb.barrier()

        cut(9)
        kb.wait_all_dma("sync")
        kb.wait_all_dma("gpsimd")
        ln2ct, evG, vG = {}, {}, {}
        for it in range(8):
            qx = "sync" if it % 2 == 0 else "gpsimd"
            if it >= 2:
                kb.wait(qx, "sV", ln2ct[it - 2])
            ld = kb.dma(qx, lambda e, it=it: e.dma_start(
                out=XT[it % 2], in_=res_d.ap()[it * 128:(it + 1) * 128, :]))
            kb.wait("vector", *ld)
            kb.wait("scalar", *ld)
            xh = XHB[:, (it % 2) * 2048:(it % 2) * 2048 + 2048]
            if it >= 2:
                kb.wait("vector", "sT", evG[it - 2])
            ln2ct[it] = ln_tile(XT[it % 2], xh, 0)
            warmT(ln2ct[it])
            kb.wait("tensor", "sV", ln2ct[it])
            for half in range(2):
                src = xh[:, half * 1024:half * 1024 + 1024]
                dst = lambda n, half=half, it=it: (
                    A(BIG1, half * 8 * 1024 + it * 128, [[16384, 128], [1024, 8], [1, 128]]),
                    PT[half][:, 0:1024])
                tp_group(src, 8, half, dst)
            evG[it] = kb.cnt["sT"]
            vG[it] = kb.cnt["sV"]

        cut(10)
        def fuse_ap(c, qb):
            if c < 32:
                return BIG0[:, c * 1024 + qb * 512:c * 1024 + qb * 512 + 512]
            return BIG2[:, (c - 32) * 1024 + qb * 512:(c - 32) * 1024 + qb * 512 + 512]

        fuse_ct = {}
        gu_mm = {}
        sup_by = {}
        last_fu = [None]
        for io in range(48):
            bufp = io % 2
            if io >= 2:
                kb.wait("gpsimd", "sT", gu_mm[io - 2])
            gin = A(gatew, io * 128, [[6144, 128], [128 * 6144, 16], [1, 128]])
            uin = A(upw, io * 128, [[6144, 128], [128 * 6144, 16], [1, 128]])
            g1 = kb.dma("gpsimd", lambda e, gin=gin, bufp=bufp: e.dma_start(
                out=W8[:, bufp * 2048:bufp * 2048 + 2048], in_=gin))
            g2 = kb.dma("gpsimd", lambda e, uin=uin, bufp=bufp: e.dma_start(
                out=W8[:, 4096 + bufp * 2048:4096 + bufp * 2048 + 2048], in_=uin))
            kb.wait("tensor", *g1)
            kb.wait("tensor", *g2)
            for qb in range(2):
                gi = io * 2 + qb
                pb = 2 * (gi % 2)
                if io == 0:
                    kb.wait("tensor", "sV", vG[3] if qb == 0 else vG[7])
                prev = sup_by.get(gi - 2)
                if prev:
                    kb.wait("tensor", "sS", prev)
                for c in range(16):
                    mmg = kb.op("tensor", lambda e, c=c, qb=qb, bufp=bufp, pb=pb: e.matmul(
                        PS[pb][:, :], W8[:, bufp * 2048 + c * 128:bufp * 2048 + (c + 1) * 128],
                        BIG1[:, c * 1024 + qb * 512:c * 1024 + qb * 512 + 512],
                        start=(c == 0), stop=(c == 15)), inc="sT" if c == 15 else None)
                for c in range(16):
                    mmu = kb.op("tensor", lambda e, c=c, qb=qb, bufp=bufp, pb=pb: e.matmul(
                        PS[pb + 1][:, :],
                        W8[:, 4096 + bufp * 2048 + c * 128:4096 + bufp * 2048 + (c + 1) * 128],
                        BIG1[:, c * 1024 + qb * 512:c * 1024 + qb * 512 + 512],
                        start=(c == 0), stop=(c == 15)), inc="sT" if c == 15 else None)
                gu_mm[io] = mmu
                kb.wait("scalar", "sT", mmg)
                if last_fu[0]:
                    kb.wait("scalar", "sV", last_fu[0])
                s_g = kb.op("scalar", lambda e, io=io, pb=pb: e.activation(
                    PB[:, 0:512], PS[pb][:, :], AFT.Silu, scale=GSCc[:, io:io + 1]), inc="sS")
                kb.wait("scalar", "sT", mmu)
                s_up = kb.op("scalar", lambda e, io=io, pb=pb: e.activation(
                    PB[:, 512:1024], PS[pb + 1][:, :], AFT.Identity, scale=USCc[:, io:io + 1]), inc="sS")
                sup_by[gi] = s_up
                kb.wait("vector", "sS", s_up)
                fuse_ct[(io, qb)] = kb.op("vector", lambda e, io=io, qb=qb: e.tensor_mul(
                    fuse_ap(io, qb), PB[:, 0:512], PB[:, 512:1024]), inc="sV")
                last_fu[0] = fuse_ct[(io, qb)]
        cut(11)
        dn_mm3 = {}
        dn_ev = {}
        dn_tct = {}
        dn_st = {}
        for dt in range(16):
            lds = {}
            for t3 in range(3):
                L = dt * 3 + t3
                slot = L % 4
                qd = "gpsimd"
                if L >= 4:
                    dtp, t3p = divmod(L - 4, 3)
                    kb.wait(qd, "sT", dn_mm3[(dtp, 1, t3p)])
                else:
                    kb.wait(qd, "sT", gu_mm[46 + (L % 2)])
                din_ap = A(downw, (t3 * 16) * 128 * 2048 + dt * 128,
                           [[2048, 128], [128 * 2048, 16], [1, 128]])
                lds[t3] = (slot, kb.dma(qd, lambda e, din_ap=din_ap, slot=slot: e.dma_start(
                    out=W8[:, slot * 2048:slot * 2048 + 2048], in_=din_ap)))
            for qb in range(2):
                gi = dt * 2 + qb
                pi = gi % 2
                pst = PS[4 + gi % 2]
                ate = F32A[:, 2048 + (gi % 2) * 512:2048 + (gi % 2) * 512 + 512]
                rslot = F32A[:, 3072:3584] if gi % 2 == 0 else F32A[:, 0:512]
                aout = F32A[:, 3584:4096] if gi % 2 == 0 else F32A[:, 1536:2048]
                prev = dn_ev.get(gi - 2)
                if prev:
                    kb.wait("tensor", "sV", prev)
                if dt == 0:
                    kb.wait("tensor", "sV", fuse_ct[(47, qb)])
                    kb.wait("tensor", "sS", sup_by[95])
                for t3 in range(3):
                    slot, ld3 = lds[t3]
                    kb.wait("tensor", *ld3)
                    for cc in range(16):
                        c = t3 * 16 + cc
                        dn_mm3[(dt, qb, t3)] = kb.op("tensor", lambda e, slot=slot, cc=cc, c=c, qb=qb,
                                                     pi=pi, isf=(t3 == 0 and cc == 0),
                                                     isl=(t3 == 2 and cc == 15): e.matmul(
                            PS[pi][:, :], W8[:, slot * 2048 + cc * 128:slot * 2048 + (cc + 1) * 128],
                            fuse_ap(c, qb), start=isf, stop=isl),
                            inc="sT" if cc == 15 else None)
                mmd = dn_mm3[(dt, qb, 2)]
                kb.wait("scalar", "sT", mmd)
                prev_tct = dn_tct.get(gi - 2)
                if prev_tct:
                    kb.wait("scalar", "sT", prev_tct)
                v_dn = kb.op("scalar", lambda e, pi=pi, dt=dt, ate=ate: e.activation(
                    ate, PS[pi][:, :], AFT.Identity, scale=DSCc[:, dt:dt + 1]), inc="sS")
                rin = A(res_d, (qb * 512) * 2048 + dt * 128, [[2048, 128], [128 * 2048, 4], [1, 128]])
                if gi < 2:
                    kb.wait("sync", "sV", ln2ct[7])
                prev_ev = dn_ev.get(gi - 2)
                if prev_ev:
                    kb.wait("sync", "sV", prev_ev)
                ld_blk = kb.dma("sync", lambda e, rin=rin, rslot=rslot: e.dma_start(
                    out=rslot, in_=rin))
                kb.wait("tensor", "sS", v_dn)
                if prev_ev:
                    kb.wait("tensor", "sV", prev_ev)
                for tq in range(4):
                    tct = kb.op("tensor", lambda e, tq=tq, ate=ate, pst=pst: e.matmul(
                        pst[:, tq * 128:(tq + 1) * 128], ate[:, tq * 128:(tq + 1) * 128], IDF[:, :],
                        is_transpose=True, start=(tq == 0), stop=(tq == 3)),
                        inc="sT" if tq == 3 else None)
                dn_tct[gi] = tct
                kb.wait("vector", "sT", tct)
                kb.wait("vector", *ld_blk)
                if (gi - 2) in dn_st:
                    kb.wait("vector", *dn_st[gi - 2])
                elif prev_store[0] and gi < 2:
                    kb.wait("vector", *prev_store[0])
                dn_ev[gi] = kb.op("vector", lambda e, pst=pst, aout=aout, rslot=rslot: e.tensor_add(
                    aout, pst[:, 0:512], rslot), inc="sV")
                kb.wait("sync", "sV", dn_ev[gi])
                oout = A(out_d, (qb * 512) * 2048 + dt * 128, [[2048, 128], [128 * 2048, 4], [1, 128]])
                dn_st[gi] = kb.dma("sync", lambda e, oout=oout, aout=aout: e.dma_start(
                    out=oout, in_=aout))

    except _Cut:
        pass
    kb.wait_all_dma("sync")
    kb.wait_all_dma("gpsimd")

    blk = es.enter_context(nc.Block())
    for name in ENGINES:
        def body(eng, name=name):
            for fn in kb.ops[name]:
                fn(eng)
        getattr(blk, name)(body)
    es.close()
    return nc


_NC = [None]


def kernel(L0_attn_x, L0_attn_freqs, L0_attn_wq_w, L0_attn_wq_b, L0_attn_wkv_a_w,
           L0_attn_wkv_b_w, L0_attn_norm_w, L0_attn_out_w, L0_attn_out_b,
           L0_ffn_gate_w, L0_ffn_up_w, L0_ffn_down_w, L0_ffn_gate_scale,
           L0_ffn_up_scale, L0_ffn_down_scale):
    f32 = lambda a: np.ascontiguousarray(np.asarray(a, dtype=np.float32))
    x = f32(L0_attn_x)
    cols = lambda v, n: np.ascontiguousarray(f32(v).reshape(n, 128).T)
    tr = lambda a: np.ascontiguousarray(f32(a).T)
    shared = {
        "wqT": tr(L0_attn_wq_w), "wkvaT": tr(L0_attn_wkv_a_w), "wkvbT": tr(L0_attn_wkv_b_w),
        "outwT": tr(L0_attn_out_w), "gatew": f32(L0_ffn_gate_w), "upw": f32(L0_ffn_up_w),
        "downw": f32(L0_ffn_down_w),
        "wqb_c": cols(L0_attn_wq_b, 16), "outb_c": cols(L0_attn_out_b, 16),
        "normw_c": cols(L0_attn_norm_w, 4), "gsc_c": cols(L0_ffn_gate_scale, 48),
        "usc_c": cols(L0_ffn_up_scale, 48), "dsc_c": cols(L0_ffn_down_scale, 16),
        "c2": np.ascontiguousarray(f32(L0_attn_freqs)[:16].reshape(16, 64).T),
        "m2": _m2_np(), "identb": np.eye(128).astype(bfd),
        "identf": np.eye(128, dtype=np.float32),
        "onesc": np.ones((128, 1), dtype=bfd),
        "onesr": np.ones((1, 128), dtype=np.float32),
        "masks": _masks_np(),
    }
    in_maps = []
    for c in range(8):
        b, h = c // 2, c % 2
        xbv = x[b] if h == 0 else np.ascontiguousarray(
            np.concatenate([x[b, 1024:], x[b, :1024]], axis=0))
        m = dict(shared)
        m["xb"] = xbv
        m["mbias"] = _mbias_np(h)
        in_maps.append(m)
    if _NC[0] is None:
        _NC[0] = build_nc()
    res = run_bass_kernel_spmd(_NC[0], in_maps, core_ids=list(range(8)))
    y = np.empty((4, 2048, 2048), dtype=np.float32)
    for c in range(8):
        b, h = c // 2, c % 2
        y[b, h * 1024:(h + 1) * 1024] = res.results[c]["out"]
    return y


def _m2_np():
    m2 = np.zeros((64, 128), dtype=np.float32)
    for i in range(32):
        for cc in range(2):
            for d in range(2):
                m2[2 * i + cc, 64 + 2 * i + d] = 1.0
    return m2


def _masks_np():
    i = np.arange(128)[:, None]
    j = np.arange(512)[None, :]
    ms = [(i + m * 128 <= j).astype(np.float32) for m in range(4)]
    return np.ascontiguousarray(np.concatenate(ms, axis=1).astype(bfd))


def _mbias_np(h):
    mb = np.zeros((2, 16), dtype=np.float32)
    if h == 0:
        mb[0, 4:16] = NEG
        mb[1, 8:16] = NEG
    flat = mb.reshape(32)
    return np.ascontiguousarray(np.repeat(flat[None, :], 128, axis=0))


# version 27
# speedup vs baseline: 1.0255x; 1.0255x over previous
import os
import numpy as np
import ml_dtypes
import contextlib
import concourse.bass as bass
import concourse.mybir as mybir
from concourse.bass_utils import run_bass_kernel_spmd
from concourse.alu_op_type import AluOpType as Op

F32 = mybir.dt.float32
BF16 = mybir.dt.bfloat16
FP8 = mybir.dt.float8e4
AFT = mybir.ActivationFunctionType
AX = mybir.AxisListType
bfd = ml_dtypes.bfloat16

D_, H_, DL = 2048, 16, 512
SS = 1.0 / float(np.sqrt(128.0))
NEG = -1.0e9
ENGINES = ("sync", "gpsimd", "vector", "scalar", "tensor")


class KB:
    def __init__(self, nc):
        self.nc = nc
        self.ops = {e: [] for e in ENGINES}
        self.cnt = {}
        self.semh = {}
        self.lastwait = {}
        self.pt_ev = [0, 0]
        self.dma_pools = {}
        self.dma_rr = {}

    def init_pools(self, es, queues=("sync", "gpsimd"), n=6):
        for q in queues:
            self.dma_pools[q] = []
            for i in range(n):
                name = f"d{q[0]}{i}"
                self.sem(name, es)
                self.dma_pools[q].append(name)
            self.dma_rr[q] = 0

    def dma(self, queue, fn):
        name = self.dma_pools[queue][self.dma_rr[queue]]
        self.dma_rr[queue] = (self.dma_rr[queue] + 1) % len(self.dma_pools[queue])
        self.wait(queue, name, self.cnt[name])
        v = self.op(queue, fn, inc=name, n=16)
        return (name, v)

    def wait_all_dma(self, eng):
        for q, pool in self.dma_pools.items():
            for name in pool:
                self.wait(eng, name, self.cnt[name])

    def sem(self, name, es):
        self.semh[name] = es.enter_context(self.nc.semaphore(name))
        self.cnt[name] = 0

    def op(self, eng, fn, inc=None, n=1):
        if inc is None:
            self.ops[eng].append(fn)
            return None
        self.cnt[inc] += n
        v = self.cnt[inc]
        h = self.semh[inc]

        def w(e, fn=fn, h=h, n=n):
            fn(e).then_inc(h, n)

        self.ops[eng].append(w)
        return v

    def wait(self, eng, name, v):
        if v is None or v <= 0:
            return
        key = (eng, name)
        if self.lastwait.get(key, 0) >= v:
            return
        self.lastwait[key] = v
        h = self.semh[name]
        self.ops[eng].append(lambda e, h=h, v=v: e.wait_ge(h, v))

    def barrier(self):
        snap = dict(self.cnt)
        for eng in ENGINES:
            for s, v in snap.items():
                self.wait(eng, s, v)


class _Cut(Exception):
    pass


def build_nc():
    PH = float(os.environ.get("KPH", "99"))

    def cut(n):
        if PH <= n:
            raise _Cut()

    nc = bass.Bass("TRN2", target_bir_lowering=False)

    def din(name, shape, dt=F32):
        return nc.dram_tensor(name, shape, dt, kind="ExternalInput")

    xb = din("xb", [2048, 2048])
    wqT = din("wqT", [2048, 2048])
    wkvaT = din("wkvaT", [2048, 512])
    wkvbT = din("wkvbT", [512, 4096])
    outwT = din("outwT", [2048, 2048])
    gatew = din("gatew", [2048, 6144])
    upw = din("upw", [2048, 6144])
    downw = din("downw", [6144, 2048])
    wqb_c = din("wqb_c", [128, 16])
    outb_c = din("outb_c", [128, 16])
    normw_c = din("normw_c", [128, 4])
    gsc_c = din("gsc_c", [128, 48])
    usc_c = din("usc_c", [128, 48])
    dsc_c = din("dsc_c", [128, 16])
    c2_d = din("c2", [64, 16])
    m2_d = din("m2", [64, 128])
    identb_d = din("identb", [128, 128], BF16)
    identf_d = din("identf", [128, 128], F32)
    onesc_d = din("onesc", [128, 1], BF16)
    onesr_d = din("onesr", [1, 128], F32)
    masks_d = din("masks", [128, 2048], BF16)
    mbias_d = din("mbias", [128, 32], F32)
    out_d = nc.dram_tensor("out", [1024, 2048], F32, kind="ExternalOutput")
    res_d = nc.dram_tensor("res_scr", [1024, 2048], F32, kind="Internal")
    lrow_d = nc.dram_tensor("lrow_scr", [32, 512], F32, kind="Internal")
    linv_d = nc.dram_tensor("linv_scr", [32, 512], F32, kind="Internal")

    es = contextlib.ExitStack()
    sb = lambda n, sh, dt: es.enter_context(nc.sbuf_tensor(n, sh, dt))
    pt = lambda n, sh, dt: es.enter_context(nc.psum_tensor(n, sh, dt))

    BIG0 = sb("BIG0", [128, 32768], BF16)
    BIG1 = sb("BIG1", [128, 16384], BF16)
    BIG2 = sb("BIG2", [128, 16384], BF16)
    LATT = sb("LATT", [128, 8192], BF16)
    XHB = sb("XHB", [128, 4096], BF16)
    WT = sb("WT", [128, 4096], BF16)
    PB = sb("PB", [128, 1024], BF16)
    SQB = sb("SQB", [128, 2048], BF16)
    F32A = sb("F32A", [128, 4096], F32)
    W8 = sb("W8", [128, 8192], FP8)
    MASKS = sb("MASKS", [128, 2048], BF16)
    WQBc = sb("WQBc", [128, 16], F32)
    OUTBc = sb("OUTBc", [128, 16], F32)
    NORMWc = sb("NORMWc", [128, 4], F32)
    GSCc = sb("GSCc", [128, 48], F32)
    USCc = sb("USCc", [128, 48], F32)
    DSCc = sb("DSCc", [128, 16], F32)
    C2s = sb("C2s", [64, 16], F32)
    M2s = sb("M2s", [64, 128], F32)
    IDB = sb("IDB", [128, 128], BF16)
    IDF = sb("IDF", [128, 128], F32)
    ONESCb = sb("ONESCb", [128, 1], BF16)
    ONESRf = sb("ONESRf", [1, 128], F32)
    MBIASs = sb("MBIASs", [128, 32], F32)
    WROPE = sb("WROPE", [128, 16], F32)
    WQBG = sb("WQBG", [128, 16], F32)
    RSTD = sb("RSTD", [128, 16], F32)
    SCL = sb("SCL", [128, 16], F32)
    SCLSQ = sb("SCLSQ", [128, 16], F32)
    BIASC = sb("BIASC", [128, 32], F32)
    BNS = sb("BNS", [128, 24], F32)
    AGG = sb("AGG", [128, 16], F32)
    STAT = sb("STAT", [128, 8], F32)
    KC = sb("KC", [128, 16], F32)
    MAXQ = sb("MAXQ", [128, 1], F32)
    MAXK = sb("MAXK", [128, 1], F32)
    ROWA = sb("ROWA", [1, 128], F32)
    MUx = sb("MUx", [1, 4], F32)
    MUCOL = sb("MUCOL", [128, 1], F32)
    LINV = sb("LINV", [1, 512], F32)
    EPSC = sb("EPSC", [128, 2], F32)
    SEL = sb("SEL", [32, 384], F32)
    X4 = sb("X4", [128, 2048], BF16)

    PS = [pt(f"PS{i}", [128, 512], F32) for i in range(6)]
    PT = [pt("PT0", [128, 1024], BF16), pt("PT1", [128, 1024], BF16)]

    kb = KB(nc)
    for s in ("sT", "sV", "sS", "sG"):
        kb.sem(s, es)
    kb.init_pools(es)
    A = bass.AP

    def tp_group(src, n, pti, dst, src_f32=False):
        kb.wait("tensor", "sV", kb.pt_ev[pti])
        for j in range(n):
            tct = kb.op("tensor", lambda e, j=j: e.matmul(
                PT[pti][:, j * 128:(j + 1) * 128], src[:, j * 128:(j + 1) * 128], IDB[:, :],
                is_transpose=True, start=(j == 0), stop=(j == n - 1)),
                inc="sT" if j == n - 1 else None)
        kb.wait("vector", "sT", tct)
        out_ap, in_ap = dst(n)
        ev = kb.op("vector", lambda e, o=out_ap, i=in_ap: e.tensor_copy(o, i), inc="sV")
        kb.pt_ev[pti] = ev
        return ev

    def strided_dst(base, off, n, chunk_stride, width_elems):
        out_ap = A(base, off, [[width_elems, 128], [chunk_stride, n], [1, 128]])
        return out_ap

    def load_strip(src_ap, buf, n_t, wait_mm=None):
        if wait_mm:
            kb.wait("gpsimd", "sT", wait_mm)
        return kb.dma("gpsimd", lambda e, s=src_ap, buf=buf, n_t=n_t: e.dma_start(
            out=WT[:, buf * 2048:buf * 2048 + n_t * 128], in_=s))

    try:
        XT = [F32A[:, 0:2048], F32A[:, 2048:4096]]
        consts = [(IDB, identb_d), (M2s, m2_d), (C2s, c2_d), (WQBc, wqb_c),
                  (IDF, identf_d), (ONESCb, onesc_d), (ONESRf, onesr_d),
                  (MASKS, masks_d), (MBIASs, mbias_d),
                  (OUTBc, outb_c), (NORMWc, normw_c), (GSCc, gsc_c),
                  (USCc, usc_c), (DSCc, dsc_c)]
        cld = {}
        for dst0, src0 in consts[:4]:
            cld[dst0.name] = kb.dma("sync", lambda e, d=dst0, s=src0: e.dma_start(
                out=d[:, :], in_=s.ap()[:, :]))
        preA = {it: kb.dma("sync" if it == 0 else "gpsimd", lambda e, it=it: e.dma_start(
            out=XT[it % 2], in_=xb.ap()[it * 128:(it + 1) * 128, :])) for it in (0, 1)}
        for dst0, src0 in consts[4:]:
            cld[dst0.name] = kb.dma("sync", lambda e, d=dst0, s=src0: e.dma_start(
                out=d[:, :], in_=s.ap()[:, :]))
        kb.op("vector", lambda e: e.memset(EPSC[:, 0:1], 1.0e-5))
        kb.op("vector", lambda e: e.memset(EPSC[:, 1:2], 1.0e-6))
        kb.op("gpsimd", lambda e: e.iota(
            SEL[0:32, 256:384], [[0, 128]], base=0, channel_multiplier=1,
            allow_small_or_imprecise_dtypes=True), inc="sG")
        kb.wait("vector", *cld["C2s"])
        kb.wait("vector", *cld["WQBc"])
        v_sq = kb.op("vector", lambda e: e.tensor_mul(AGG[0:64, 0:16], C2s[0:64, :], C2s[0:64, :]), inc="sV")
        kb.wait("tensor", "sV", v_sq)
        kb.wait("tensor", *cld["M2s"])
        kb.wait("tensor", *cld["IDB"])
        t_g = kb.op("tensor", lambda e: e.matmul(PS[5][:, 0:16], M2s[0:64, :], AGG[0:64, 0:16],
                                                 start=True, stop=True), inc="sT")
        kb.op("vector", lambda e: e.memset(WROPE[0:64, 0:16], 1.0))
        kb.wait("vector", "sT", t_g)
        vw = kb.op("vector", lambda e: e.tensor_copy(WROPE[64:128, 0:16], PS[5][64:128, 0:16]), inc="sV")
        kb.wait("vector", "sV", vw)
        kb.op("vector", lambda e: e.tensor_mul(WQBG[:, 0:16], WROPE[:, 0:16], WQBc[:, 0:16]))

        cut(1)
        strips_B = {dl: load_strip(A(wkvaT, dl * 128, [[512, 128], [128 * 512, 16], [1, 128]]),
                                   dl % 2, 16) for dl in (0, 1)}
        def vedge(fn):
            v = kb.op("vector", fn, inc="sV")
            kb.wait("vector", "sV", v)
            return v

        def warmT(upto, cap=5):
            lo = max(kb.lastwait.get(("tensor", "sV"), 0) + 1, upto - cap)
            for v in range(lo, upto):
                kb.wait("tensor", "sV", v)
                kb.op("tensor", lambda e: e.matmul(PS[5][0:1, 0:1], IDB[:, 0:1], IDB[:, 0:1],
                                                   start=True, stop=True))

        def ln_tile(buf, xh, eps_col):
            kb.op("scalar", lambda e, buf=buf: e.activation(
                BIG2[:, 0:2048], buf, AFT.Identity, accum_out=STAT[:, 0:1]), inc="sS")
            s_sq = kb.op("scalar", lambda e, buf=buf: e.activation(
                SQB[:, 0:2048], buf, AFT.Square, accum_out=STAT[:, 1:2]), inc="sS")
            kb.wait("vector", "sS", s_sq)
            vedge(lambda e: e.tensor_scalar(STAT[:, 2:3], STAT[:, 0:1], 1.0 / 2048, None, Op.mult))
            vedge(lambda e: e.tensor_mul(STAT[:, 3:4], STAT[:, 2:3], STAT[:, 2:3]))
            v_var = kb.op("vector", lambda e: e.scalar_tensor_tensor(
                STAT[:, 5:6], STAT[:, 1:2], 1.0 / 2048, STAT[:, 3:4],
                Op.mult, Op.subtract), inc="sV")
            kb.wait("vector", "sV", v_var)
            kb.wait("scalar", "sV", v_var)
            s_sd = kb.op("scalar", lambda e, eps_col=eps_col: e.activation(
                STAT[:, 6:7], STAT[:, 5:6], AFT.Sqrt, bias=EPSC[:, eps_col:eps_col + 1],
                scale=1.0), inc="sS")
            kb.wait("vector", "sS", s_sd)
            vedge(lambda e: e.reciprocal(STAT[:, 7:8], STAT[:, 6:7]))
            return kb.op("vector", lambda e, xh=xh, buf=buf: e.tensor_scalar(
                xh, buf, STAT[:, 2:3], STAT[:, 7:8], Op.subtract, Op.mult), inc="sV")

        lnct, evA = {}, {}
        for it in range(16):
            if it < 2:
                ld = preA[it]
            else:
                qx = "sync" if it % 2 == 0 else "gpsimd"
                kb.wait(qx, "sV", lnct[it - 2])
                ld = kb.dma(qx, lambda e, it=it: e.dma_start(
                    out=XT[it % 2], in_=xb.ap()[it * 128:(it + 1) * 128, :]))
            kb.wait("vector", *ld)
            kb.wait("scalar", *ld)
            xh = XHB[:, (it % 2) * 2048:(it % 2) * 2048 + 2048]
            if it >= 2:
                kb.wait("vector", "sT", evA[it - 2])
            lnct[it] = ln_tile(XT[it % 2], xh, 0)
            warmT(lnct[it])
            kb.wait("tensor", "sV", lnct[it])
            for half in range(2):
                src = xh[:, half * 1024:half * 1024 + 1024]
                dst = lambda n, half=half, it=it, pti=half: (
                    A(BIG0, half * 8 * 2048 + it * 128, [[32768, 128], [2048, 8], [1, 128]]),
                    PT[half][:, 0:1024])
                evA[it] = tp_group(src, 8, half, dst)
            evA[it] = kb.cnt["sT"]
        kb.barrier()

        cut(2)
        first_col = [True]
        lat_ev = {}
        mm_lat = {}
        col_prev = None
        pending = None

        def emit_cols(idx, tb, sqct, is_last):
            kb.wait("tensor", "sS", sqct)
            slot = idx % 2
            cp = None
            for tt in range(4):
                isf = first_col[0]
                first_col[0] = False
                isl = (is_last and tt == 3)
                cp = kb.op("tensor", lambda e, tt=tt, tb=tb, slot=slot, isf=isf, isl=isl: e.matmul(
                    PS[4][:, tb * 4 + tt:tb * 4 + tt + 1],
                    SQB[:, slot * 512 + tt * 128:slot * 512 + (tt + 1) * 128], ONESCb[:, 0:1],
                    start=isf, stop=isl), inc="sT")
            return cp

        for dl in range(4):
            if dl < 2:
                strip = strips_B[dl]
            else:
                strip = load_strip(A(wkvaT, dl * 128, [[512, 128], [128 * 512, 16], [1, 128]]),
                                   dl % 2, 16, wait_mm=mm_lat.get(dl - 2))
            if dl == 0:
                cut(2.2)
            kb.wait("tensor", *strip)
            for tb in range(4):
                idx = dl * 4 + tb
                pi = idx % 4
                prev = lat_ev.get(idx - 4)
                if prev:
                    kb.wait("tensor", "sS", prev)
                for c in range(16):
                    mmct = kb.op("tensor", lambda e, c=c, pi=pi, dl=dl, tb=tb: e.matmul(
                        PS[pi][:, :], WT[:, (dl % 2) * 2048 + c * 128:(dl % 2) * 2048 + (c + 1) * 128],
                        BIG0[:, c * 2048 + tb * 512:c * 2048 + tb * 512 + 512],
                        start=(c == 0), stop=(c == 15)), inc="sT" if c == 15 else None)
                mm_lat[dl] = mmct
                if dl == 0 and tb == 0:
                    cut(2.3)
                kb.wait("scalar", "sT", mmct)
                if col_prev:
                    kb.wait("scalar", "sT", col_prev)
                v_sq2 = kb.op("scalar", lambda e, pi=pi, idx=idx: e.activation(
                    SQB[:, (idx % 2) * 512:(idx % 2) * 512 + 512], PS[pi][:, :],
                    AFT.Square), inc="sS")
                if dl == 0 and tb == 0:
                    cut(2.45)
                lat_ev[idx] = kb.op("scalar", lambda e, pi=pi, dl=dl, tb=tb: e.activation(
                    LATT[:, dl * 2048 + tb * 512:dl * 2048 + tb * 512 + 512],
                    PS[pi][:, :], AFT.Identity, scale=NORMWc[:, dl:dl + 1]), inc="sS")
                if dl == 0 and tb == 0:
                    cut(2.6)
                if pending is not None:
                    col_prev = emit_cols(*pending, is_last=False)
                pending = (idx, tb, v_sq2)
        strips_C = {ot: load_strip(A(wqT, ot * 128, [[2048, 128], [128 * 2048, 16], [1, 128]]),
                                   ot % 2, 16, wait_mm=mm_lat[2 + ot]) for ot in (0, 1)}
        col_prev = emit_cols(*pending, is_last=True)
        kb.wait("scalar", "sT", col_prev)
        s_sr = kb.op("scalar", lambda e: e.activation(RSTD[:, 0:16], PS[4][:, 0:16], AFT.Sqrt,
                                                      bias=EPSC[:, 1:2], scale=1.0 / 512), inc="sS")
        kb.wait("vector", "sS", s_sr)
        vedge(lambda e: e.reciprocal(RSTD[:, 0:16], RSTD[:, 0:16]))
        vedge(lambda e: e.tensor_scalar(SCL[:, 0:16], RSTD[:, 0:16], SS, None, Op.mult))
        vedge(lambda e: e.tensor_mul(SCLSQ[:, 0:16], SCL[:, 0:16], SCL[:, 0:16]))
        kb.barrier()

        cut(3)
        q_ev = {}
        mm_q = {}
        qcol_by = {}
        qn_red = {}
        v_mq_box = [None]

        def emit_qnorm(h):
            sqb_h = SQB if h % 2 == 0 else X4
            kb.wait("vector", "sS", q_ev[h * 2 + 1])
            if h >= 2:
                kb.wait("vector", "sT", qcol_by[h - 2])
            v_qsq = kb.op("vector", lambda e, h=h, sqb_h=sqb_h: e.tensor_mul(
                sqb_h[:, 0:1024], BIG1[:, h * 1024:(h + 1) * 1024],
                BIG1[:, h * 1024:(h + 1) * 1024]), inc="sV")
            kb.wait("tensor", "sV", v_qsq)
            if h >= 1:
                kb.wait("tensor", "sV", qn_red[h - 1])
            for tq in range(8):
                qcol = kb.op("tensor", lambda e, tq=tq, sqb_h=sqb_h: e.matmul(
                    PS[4][:, tq:tq + 1], sqb_h[:, tq * 128:(tq + 1) * 128], ONESCb[:, 0:1],
                    start=(tq == 0), stop=(tq == 7)), inc="sT" if tq == 7 else None)
            qcol_by[h] = qcol
            kb.wait("vector", "sT", qcol)
            qn_red[h] = vedge(lambda e: e.tensor_reduce(STAT[:, 0:1], PS[4][:, 0:8], AX.X, Op.max))
            if h == 0:
                v_mq_box[0] = vedge(lambda e: e.tensor_copy(MAXQ[:, 0:1], STAT[:, 0:1]))
            else:
                v_mq_box[0] = vedge(lambda e: e.tensor_max(MAXQ[:, 0:1], MAXQ[:, 0:1], STAT[:, 0:1]))

        for ot in range(16):
            if ot < 2:
                strip = strips_C[ot]
            else:
                strip = load_strip(A(wqT, ot * 128, [[2048, 128], [128 * 2048, 16], [1, 128]]),
                                   ot % 2, 16, wait_mm=mm_q.get(ot - 2))
            kb.wait("tensor", *strip)
            for qb in range(2):
                pi = (ot * 2 + qb) % 4
                prev = q_ev.get(ot * 2 + qb - 4)
                if prev:
                    kb.wait("tensor", "sS", prev)
                for c in range(16):
                    mmct = kb.op("tensor", lambda e, c=c, pi=pi, ot=ot, qb=qb: e.matmul(
                        PS[pi][:, :], WT[:, (ot % 2) * 2048 + c * 128:(ot % 2) * 2048 + (c + 1) * 128],
                        BIG0[:, c * 2048 + qb * 512:c * 2048 + qb * 512 + 512],
                        start=(c == 0), stop=(c == 15)), inc="sT" if c == 15 else None)
                mm_q[ot] = mmct
                kb.wait("scalar", "sT", mmct)
                q_ev[ot * 2 + qb] = kb.op("scalar", lambda e, pi=pi, ot=ot, qb=qb: e.activation(
                    BIG1[:, ot * 1024 + qb * 512:ot * 1024 + qb * 512 + 512],
                    PS[pi][:, :], AFT.Identity, bias=WQBG[:, ot:ot + 1],
                    scale=WROPE[:, ot:ot + 1]), inc="sS")
            if ot >= 1:
                emit_qnorm(ot - 1)
        emit_qnorm(15)
        v_mq = v_mq_box[0]
        kb.wait("gpsimd", "sV", v_mq)
        g_mq = kb.dma("gpsimd", lambda e: e.dma_start(out=ROWA[0:1, 0:128], in_=MAXQ[0:128, 0:1],
                                                      allow_slow_non_contiguous=True))
        kb.wait("vector", *g_mq)
        kb.op("vector", lambda e: e.tensor_reduce(MUx[0:1, 0:1], ROWA[0:1, 0:128], AX.X, Op.max))
        kb.barrier()

        cut(4)
        KOFF, VOFF = 0, 16384
        slot_lists = {0: [0, 1, 2, 3, 8, 9, 10, 11, 12, 13, 14, 15], 1: list(range(16))}
        pair_idx = 0
        bcast_rd = {}
        wr_by = {}
        PSL = [PB[:, 0:512], PB[:, 512:1024], SQB[:, 0:512], SQB[:, 512:1024],
               X4[:, 0:512], X4[:, 512:1024], X4[:, 1024:1536], X4[:, 1536:2048]]
        PADD = [SQB[:, 1024:1536], SQB[:, 1536:2048]]
        linv_by_pair = {}
        otev_by_pair = {}
        exp_ring = {}
        av_ring = {}
        av_by = {}
        padd_ring = {}
        g_ctr = [0]
        bcast_S = [None]

        for hg in range(2):
            k_ev = {}
            mm_k = {}
            vksq_by = {}
            kcol_k = {}
            kn_red = {}
            v_mk_box = [None]
            pend_kn = [None]

            def emit_kn(i8):
                sqb_i = SQB if i8 % 2 == 0 else X4
                kb.wait("tensor", "sV", vksq_by[i8])
                if i8 >= 1:
                    kb.wait("tensor", "sV", kn_red[i8 - 1])
                for kt in range(16):
                    kcol = kb.op("tensor", lambda e, kt=kt, sqb_i=sqb_i: e.matmul(
                        PS[4][:, kt:kt + 1], sqb_i[:, kt * 128:(kt + 1) * 128], ONESCb[:, 0:1],
                        start=(kt == 0), stop=(kt == 15)), inc="sT" if kt == 15 else None)
                kcol_k[i8] = kcol
                kb.wait("vector", "sT", kcol)
                kn_red[i8] = vedge(lambda e: e.tensor_mul(KC[:, 0:16], PS[4][:, 0:16], SCLSQ[:, 0:16]))
                vedge(lambda e: e.tensor_reduce(STAT[:, 0:1], KC[:, 0:16], AX.X, Op.max))
                if i8 == 0:
                    v_mk_box[0] = vedge(lambda e: e.tensor_copy(MAXK[:, 0:1], STAT[:, 0:1]))
                else:
                    v_mk_box[0] = vedge(lambda e: e.tensor_max(MAXK[:, 0:1], MAXK[:, 0:1], STAT[:, 0:1]))

            for i8 in range(8):
                h = hg * 8 + i8
                strip = load_strip(A(wkvbT, h * 128, [[4096, 128], [128 * 4096, 4], [1, 128]]),
                                   i8 % 2, 4, wait_mm=mm_k.get(i8 - 2))
                kb.wait("tensor", *strip)
                prev = k_ev.get(i8 - 1)
                if prev:
                    kb.wait("tensor", "sV", prev)
                for tb in range(4):
                    for c in range(4):
                        mmct = kb.op("tensor", lambda e, c=c, tb=tb, i8=i8: e.matmul(
                            PS[tb][:, :], WT[:, (i8 % 2) * 2048 + c * 128:(i8 % 2) * 2048 + (c + 1) * 128],
                            LATT[:, c * 2048 + tb * 512:c * 2048 + tb * 512 + 512],
                            start=(c == 0), stop=(c == 3)),
                            inc="sT" if (tb == 3 and c == 3) else None)
                mm_k[i8] = mmct
                kb.wait("vector", "sT", mmct)
                for tb in range(4):
                    k_ev[i8] = kb.op("vector", lambda e, tb=tb, i8=i8: e.tensor_copy(
                        BIG0[:, KOFF + i8 * 2048 + tb * 512:KOFF + i8 * 2048 + tb * 512 + 512],
                        PS[tb][:, :]), inc="sV")
                kb.wait("vector", "sV", k_ev[i8])
                if i8 >= 2:
                    kb.wait("vector", "sT", kcol_k[i8 - 2])
                vksq_by[i8] = kb.op("vector", lambda e, i8=i8, sqb_i=(SQB if i8 % 2 == 0 else X4): e.tensor_mul(
                    sqb_i[:, 0:2048], BIG0[:, KOFF + i8 * 2048:KOFF + (i8 + 1) * 2048],
                    BIG0[:, KOFF + i8 * 2048:KOFF + (i8 + 1) * 2048]), inc="sV")
                if pend_kn[0] is not None:
                    emit_kn(pend_kn[0])
                pend_kn[0] = i8
            emit_kn(7)
            v_mk = v_mk_box[0]
            cut(5 - hg)
            vin = A(wkvbT, 2048 + hg * 1024, [[4096, 128], [128 * 4096, 4], [1, 1024]])
            wv_ld = kb.dma("gpsimd", lambda e, vin=vin: e.dma_start(
                out=A(XHB, 0, [[4096, 128], [1024, 4], [1, 1024]]), in_=vin))
            kb.wait("gpsimd", "sV", v_mk)
            g_mk = kb.dma("gpsimd", lambda e: e.dma_start(out=ROWA[0:1, 0:128], in_=MAXK[0:128, 0:1],
                                                          allow_slow_non_contiguous=True))
            kb.wait("vector", *g_mk)
            vedge(lambda e: e.tensor_reduce(MUx[0:1, 1:2], ROWA[0:1, 0:128], AX.X, Op.max))
            v_mu2 = vedge(lambda e: e.tensor_mul(MUx[0:1, 2:3], MUx[0:1, 0:1], MUx[0:1, 1:2]))
            kb.wait("scalar", "sV", v_mu2)
            s_mu = kb.op("scalar", lambda e: e.activation(MUx[0:1, 3:4], MUx[0:1, 2:3],
                                                          AFT.Sqrt), inc="sS")
            kb.wait("vector", "sS", s_mu)
            v_mun = kb.op("vector", lambda e: e.tensor_scalar(MUx[0:1, 3:4], MUx[0:1, 3:4],
                                                              -1.0, None, Op.mult), inc="sV")
            kb.wait("tensor", "sV", v_mun)
            t_muc = kb.op("tensor", lambda e: e.matmul(PS[5][:, 0:1], ONESRf[0:1, :], MUx[0:1, 3:4],
                                                       start=True, stop=True), inc="sT")
            kb.wait("vector", "sT", t_muc)
            vedge(lambda e: e.tensor_copy(MUCOL[:, 0:1], PS[5][:, 0:1]))
            v_bias = vedge(lambda e: e.tensor_scalar(BIASC[:, 0:32], MBIASs[:, 0:32],
                                                     MUCOL[:, 0:1], None, Op.add))
            cut(7 - hg)
            kb.wait("tensor", *wv_ld)
            kb.wait("tensor", "sV", k_ev[7])
            v_ev = {}
            vts_by = {}
            for tt in range(16):
                pb0 = (tt % 2) * 2
                if tt >= 2:
                    kb.wait("tensor", "sS", v_ev[(tt - 2, 1)])
                    kb.wait("tensor", "sV", vts_by[tt - 2])
                for dl in range(4):
                    for ovb in range(2):
                        mmv = kb.op("tensor", lambda e, tt=tt, dl=dl, ovb=ovb, pb0=pb0: e.matmul(
                            PS[pb0 + ovb][:, :], LATT[:, dl * 2048 + tt * 128:dl * 2048 + tt * 128 + 128],
                            XHB[:, dl * 1024 + ovb * 512:dl * 1024 + ovb * 512 + 512],
                            start=(dl == 0), stop=(dl == 3)),
                            inc="sT" if (dl == 3 and ovb == 1) else None)
                if tt == 0:
                    cut(5.6 - hg)
                kb.wait("vector", "sT", mmv)
                if tt >= 1:
                    kb.wait("vector", "sV", vts_by[tt - 1])
                vc = kb.op("vector", lambda e, pb0=pb0: e.tensor_copy(
                    F32A[:, 0:512], PS[pb0][:, :]), inc="sV")
                kb.wait("vector", "sV", vc)
                vts_by[tt] = kb.op("vector", lambda e, tt=tt: e.tensor_scalar(
                    BIG0[:, VOFF + tt * 1024:VOFF + tt * 1024 + 512],
                    F32A[:, 0:512], RSTD[:, tt:tt + 1], None, Op.mult), inc="sV")
                kb.wait("scalar", "sT", mmv)
                v_ev[(tt, 1)] = kb.op("scalar", lambda e, tt=tt, pb0=pb0: e.activation(
                    BIG0[:, VOFF + tt * 1024 + 512:VOFF + tt * 1024 + 1024],
                    PS[pb0 + 1][:, :], AFT.Identity, scale=RSTD[:, tt:tt + 1]), inc="sS")
            kb.barrier()
            cut(6 - hg)
            for i8 in range(8):
                h = hg * 8 + i8
                for qb in range(2):
                    slots = slot_lists[qb]
                    nsl = len(slots)
                    npair = nsl // 2
                    po = 2 + (pair_idx % 2)
                    gstart = g_ctr[0]
                    rdy = {}
                    padd_by = {}
                    lmm_ct = {}

                    def emit_avs(p, i8=i8, qb=qb, po=po, gstart=gstart, nsl=None):
                        kb.wait("tensor", *rdy[2 * p])
                        kb.wait("tensor", *rdy[2 * p + 1])
                        if p == 0 and pair_idx > 1:
                            kb.wait("tensor", "sV", otev_by_pair[pair_idx - 2])
                        for i in (2 * p, 2 * p + 1):
                            kt_i = slot_lists[qb][i]
                            pi_i = (gstart + i) % 8
                            av_by[(pair_idx, i)] = av_ring[gstart + i] = kb.op(
                                "tensor", lambda e, pi_i=pi_i, kt_i=kt_i, i8=i8,
                                isf=(i == 0), isl=(i == nsl - 1), po=po: e.matmul(
                                    PS[po][:, :],
                                    BIG0[:, VOFF + kt_i * 1024 + i8 * 128:VOFF + kt_i * 1024 + i8 * 128 + 128],
                                    PSL[pi_i],
                                    start=isf, stop=isl), inc="sT")

                    def emit_lmm(p, npair=None):
                        kb.wait("tensor", "sV", padd_by[p])
                        if p == 0 and pair_idx > 0:
                            kb.wait("tensor", "sV", linv_by_pair[pair_idx - 1])
                        lmm_ct[p] = kb.op("tensor", lambda e, p=p, isf=(p == 0),
                                          isl=(p == npair - 1): e.matmul(
                            PS[5][0:1, 0:512], ONESCb[:, 0:1], PADD[p % 2],
                            start=isf, stop=isl), inc="sT")

                    for j, kt in enumerate(slots):
                        g = gstart + j
                        pi = g % 8
                        sb_ = [PS[0], PS[1], PS[4]][g % 3]
                        if g >= 3:
                            kb.wait("tensor", "sS", exp_ring[g - 3])
                        t_s = kb.op("tensor", lambda e, sb_=sb_, i8=i8, kt=kt, qb=qb, h=h: e.matmul(
                            sb_[:, :],
                            BIG0[:, KOFF + i8 * 2048 + kt * 128:KOFF + i8 * 2048 + kt * 128 + 128],
                            BIG1[:, h * 1024 + qb * 512:h * 1024 + qb * 512 + 512],
                            start=True, stop=True), inc="sT")
                        kb.wait("scalar", "sT", t_s)
                        if g >= 8:
                            kb.wait("scalar", "sT", av_ring[g - 8])
                            kb.wait("scalar", "sV", padd_ring[g - 8])
                        exp_ring[g] = kb.op("scalar", lambda e, pi=pi, sb_=sb_, qb=qb, kt=kt: e.activation(
                            PSL[pi], sb_[:, :], AFT.Exp,
                            bias=BIASC[:, qb * 16 + kt:qb * 16 + kt + 1],
                            scale=SCL[:, kt:kt + 1]), inc="sS")
                        if 4 * qb <= kt <= 4 * qb + 3:
                            kb.wait("vector", "sS", exp_ring[g])
                            m = kt - 4 * qb
                            mk = kb.op("vector", lambda e, pi=pi, m=m: e.tensor_mul(
                                PSL[pi], PSL[pi], MASKS[:, m * 512:(m + 1) * 512]), inc="sV")
                            rdy[j] = ("sV", mk)
                        else:
                            rdy[j] = ("sS", exp_ring[g])
                        if j % 2 == 1:
                            p = j // 2
                            if p >= 1:
                                emit_avs(p - 1, nsl=nsl)
                            if p >= 2:
                                emit_lmm(p - 2, npair=npair)
                            kb.wait("vector", *rdy[j - 1])
                            kb.wait("vector", *rdy[j])
                            if p >= 2:
                                kb.wait("vector", "sT", lmm_ct[p - 2])
                            padd_by[p] = kb.op("vector", lambda e, p=p, gstart=gstart: e.tensor_add(
                                PADD[p % 2], PSL[(gstart + 2 * p) % 8],
                                PSL[(gstart + 2 * p + 1) % 8]), inc="sV")
                            padd_ring[gstart + 2 * p] = padd_by[p]
                            padd_ring[gstart + 2 * p + 1] = padd_by[p]
                    emit_avs(npair - 1, nsl=nsl)
                    emit_lmm(npair - 2, npair=npair)
                    emit_lmm(npair - 1, npair=npair)
                    g_ctr[0] += nsl
                    pgl = pair_idx % 16
                    kb.wait("vector", "sT", lmm_ct[npair - 1])
                    if (pair_idx - 1) in wr_by:
                        kb.wait("vector", *wr_by[pair_idx - 1])
                    linv_by_pair[pair_idx] = kb.op("vector", lambda e: e.tensor_copy(
                        LINV[0:1, 0:512], PS[5][0:1, 0:512]), inc="sV")
                    kb.wait("gpsimd", "sV", linv_by_pair[pair_idx])
                    wr_by[pair_idx] = kb.dma("gpsimd", lambda e, row=pair_idx: e.dma_start(
                        out=F32A[row:row + 1, 3072:3584], in_=LINV[0:1, 0:512],
                        allow_slow_non_contiguous=True))
                    otev_by_pair[pair_idx] = kb.op("vector", lambda e, po=po, h=h, qb=qb: e.tensor_copy(
                        BIG2[:, h * 1024 + qb * 512:h * 1024 + qb * 512 + 512],
                        PS[po][:, :]), inc="sV")
                    pair_idx += 1

            kb.barrier()

        cut(8)
        v_r32 = vedge(lambda e: e.reciprocal(
            F32A[0:32, 2560:3072], F32A[0:32, 3072:3584]))
        kb.wait("tensor", "sV", v_r32)
        last_n = None
        nbcs = {}
        mm_by = {}
        order = list(range(0, 32, 2)) + list(range(1, 32, 2))
        for k, p in enumerate(order):
            hq, qbp = p // 2, p % 2
            s = k % 2
            if k >= 2:
                kb.wait("vector", "sT", mm_by[order[k - 2]])
            v_sel = kb.op("vector", lambda e, p=p, s=s: e.tensor_scalar(
                SEL[0:32, s * 128:(s + 1) * 128], SEL[0:32, 256:384],
                float(p), None, Op.is_equal), inc="sV")
            kb.wait("tensor", "sV", v_sel)
            if k >= 2:
                kb.wait("tensor", "sV", nbcs[order[k - 2]])
            mm_by[p] = mmb = kb.op("tensor", lambda e, s=s: e.matmul(
                PS[2 + s][:, 0:512], SEL[0:32, s * 128:(s + 1) * 128],
                F32A[0:32, 2560:3072], start=True, stop=True), inc="sT")
            kb.wait("vector", "sT", mmb)
            last_n = kb.op("vector", lambda e, hq=hq, qbp=qbp, s=s: e.tensor_mul(
                BIG2[:, hq * 1024 + qbp * 512:hq * 1024 + qbp * 512 + 512],
                PS[2 + s][:, 0:512],
                BIG2[:, hq * 1024 + qbp * 512:hq * 1024 + qbp * 512 + 512]), inc="sV")
            nbcs[p] = last_n
        add_ct = {}
        mm_f = {}
        tctF = {}
        store_F = {}
        prev_store = [None]
        pend = [None]

        def flush_F(nxt_mm=None):
            pi, dt, qb, s_ev, ld_blk, xsl = pend[0]
            gi = dt * 2 + qb
            pst = PS[4 + gi % 2]
            aout = F32A[:, 3584:4096] if gi % 2 == 0 else F32A[:, 1024:1536]
            kb.wait("tensor", "sS", s_ev)
            pa = add_ct.get(gi - 2)
            if pa:
                kb.wait("tensor", "sV", pa[1])
            ate = F32A[:, 2048 + (gi % 2) * 512:2048 + (gi % 2) * 512 + 512]
            for tq in range(4):
                tct = kb.op("tensor", lambda e, tq=tq, ate=ate, pst=pst: e.matmul(
                    pst[:, tq * 128:(tq + 1) * 128], ate[:, tq * 128:(tq + 1) * 128], IDF[:, :],
                    is_transpose=True, start=(tq == 0), stop=(tq == 3)),
                    inc="sT" if tq == 3 else None)
            tctF[gi] = tct
            kb.wait("vector", "sT", tct)
            kb.wait("vector", *ld_blk)
            if (gi - 2) in store_F:
                kb.wait("vector", *store_F[gi - 2])
            v_add = kb.op("vector", lambda e, xsl=xsl, pst=pst, aout=aout: e.tensor_add(
                aout, pst[:, 0:512], F32A[:, xsl * 512:xsl * 512 + 512]), inc="sV")
            add_ct[gi] = (s_ev, v_add)
            kb.wait("sync", "sV", v_add)
            rout = A(res_d, (qb * 512) * 2048 + dt * 128, [[2048, 128], [128 * 2048, 4], [1, 128]])
            store_F[gi] = kb.dma("sync", lambda e, rout=rout, aout=aout: e.dma_start(
                out=rout, in_=aout))
            prev_store[0] = store_F[gi]
            pend[0] = None

        for dt in range(16):
            strip = load_strip(A(outwT, dt * 128, [[2048, 128], [128 * 2048, 16], [1, 128]]),
                               dt % 2, 16, wait_mm=mm_f.get(dt - 2))
            kb.wait("tensor", *strip)
            for qb in range(2):
                pi = (dt * 2 + qb) % 2
                prev = add_ct.get(dt * 2 + qb - 2)
                if prev:
                    kb.wait("tensor", "sS", prev[0])
                for c in range(16):
                    if dt == 0:
                        kb.wait("tensor", "sV", nbcs[c * 2 + qb])
                    mmct = kb.op("tensor", lambda e, c=c, pi=pi, dt=dt, qb=qb: e.matmul(
                        PS[pi][:, :], WT[:, (dt % 2) * 2048 + c * 128:(dt % 2) * 2048 + (c + 1) * 128],
                        BIG2[:, c * 1024 + qb * 512:c * 1024 + qb * 512 + 512],
                        start=(c == 0), stop=(c == 15)), inc="sT" if c == 15 else None)
                mm_f[dt] = mmct
                kb.wait("scalar", "sT", mmct)
                prev_tct = tctF.get(dt * 2 + qb - 1)
                if prev_tct:
                    kb.wait("scalar", "sT", prev_tct)
                ate = F32A[:, 2048 + pi * 512:2048 + pi * 512 + 512]
                s_ev = kb.op("scalar", lambda e, pi=pi, dt=dt, ate=ate: e.activation(
                    ate, PS[pi][:, :], AFT.Identity, bias=OUTBc[:, dt:dt + 1], scale=1.0), inc="sS")
                xin = A(xb, (qb * 512) * 2048 + dt * 128, [[2048, 128], [128 * 2048, 4], [1, 128]])
                xsl = (dt * 2 + qb) % 2
                prev_add = add_ct.get(dt * 2 + qb - 2)
                if prev_add:
                    kb.wait("sync", "sV", prev_add[1])
                ld_blk = kb.dma("sync", lambda e, xin=xin, xsl=xsl: e.dma_start(
                    out=F32A[:, xsl * 512:xsl * 512 + 512], in_=xin))
                if pend[0] is not None:
                    flush_F()
                pend[0] = (pi, dt, qb, s_ev, ld_blk, xsl)
        flush_F()
        kb.barrier()

        cut(9)
        kb.wait_all_dma("sync")
        kb.wait_all_dma("gpsimd")
        ln2ct, evG, vG = {}, {}, {}
        for it in range(8):
            qx = "sync" if it % 2 == 0 else "gpsimd"
            if it >= 2:
                kb.wait(qx, "sV", ln2ct[it - 2])
            ld = kb.dma(qx, lambda e, it=it: e.dma_start(
                out=XT[it % 2], in_=res_d.ap()[it * 128:(it + 1) * 128, :]))
            kb.wait("vector", *ld)
            kb.wait("scalar", *ld)
            xh = XHB[:, (it % 2) * 2048:(it % 2) * 2048 + 2048]
            if it >= 2:
                kb.wait("vector", "sT", evG[it - 2])
            ln2ct[it] = ln_tile(XT[it % 2], xh, 0)
            warmT(ln2ct[it])
            kb.wait("tensor", "sV", ln2ct[it])
            for half in range(2):
                src = xh[:, half * 1024:half * 1024 + 1024]
                dst = lambda n, half=half, it=it: (
                    A(BIG1, half * 8 * 1024 + it * 128, [[16384, 128], [1024, 8], [1, 128]]),
                    PT[half][:, 0:1024])
                tp_group(src, 8, half, dst)
            evG[it] = kb.cnt["sT"]
            vG[it] = kb.cnt["sV"]

        cut(10)
        def fuse_ap(c, qb):
            if c < 32:
                return BIG0[:, c * 1024 + qb * 512:c * 1024 + qb * 512 + 512]
            return BIG2[:, (c - 32) * 1024 + qb * 512:(c - 32) * 1024 + qb * 512 + 512]

        fuse_ct = {}
        gu_mm = {}
        sup_by = {}
        last_fu = [None]
        for io in range(48):
            bufp = io % 2
            if io >= 2:
                kb.wait("gpsimd", "sT", gu_mm[io - 2])
            gin = A(gatew, io * 128, [[6144, 128], [128 * 6144, 16], [1, 128]])
            uin = A(upw, io * 128, [[6144, 128], [128 * 6144, 16], [1, 128]])
            g1 = kb.dma("gpsimd", lambda e, gin=gin, bufp=bufp: e.dma_start(
                out=W8[:, bufp * 2048:bufp * 2048 + 2048], in_=gin))
            g2 = kb.dma("gpsimd", lambda e, uin=uin, bufp=bufp: e.dma_start(
                out=W8[:, 4096 + bufp * 2048:4096 + bufp * 2048 + 2048], in_=uin))
            kb.wait("tensor", *g1)
            kb.wait("tensor", *g2)
            for qb in range(2):
                gi = io * 2 + qb
                pb = 2 * (gi % 2)
                if io == 0:
                    kb.wait("tensor", "sV", vG[3] if qb == 0 else vG[7])
                prev = sup_by.get(gi - 2)
                if prev:
                    kb.wait("tensor", "sS", prev)
                for c in range(16):
                    mmg = kb.op("tensor", lambda e, c=c, qb=qb, bufp=bufp, pb=pb: e.matmul(
                        PS[pb][:, :], W8[:, bufp * 2048 + c * 128:bufp * 2048 + (c + 1) * 128],
                        BIG1[:, c * 1024 + qb * 512:c * 1024 + qb * 512 + 512],
                        start=(c == 0), stop=(c == 15)), inc="sT" if c == 15 else None)
                for c in range(16):
                    mmu = kb.op("tensor", lambda e, c=c, qb=qb, bufp=bufp, pb=pb: e.matmul(
                        PS[pb + 1][:, :],
                        W8[:, 4096 + bufp * 2048 + c * 128:4096 + bufp * 2048 + (c + 1) * 128],
                        BIG1[:, c * 1024 + qb * 512:c * 1024 + qb * 512 + 512],
                        start=(c == 0), stop=(c == 15)), inc="sT" if c == 15 else None)
                gu_mm[io] = mmu
                kb.wait("scalar", "sT", mmg)
                if last_fu[0]:
                    kb.wait("scalar", "sV", last_fu[0])
                s_g = kb.op("scalar", lambda e, io=io, pb=pb: e.activation(
                    PB[:, 0:512], PS[pb][:, :], AFT.Silu, scale=GSCc[:, io:io + 1]), inc="sS")
                kb.wait("scalar", "sT", mmu)
                s_up = kb.op("scalar", lambda e, io=io, pb=pb: e.activation(
                    PB[:, 512:1024], PS[pb + 1][:, :], AFT.Identity, scale=USCc[:, io:io + 1]), inc="sS")
                sup_by[gi] = s_up
                kb.wait("vector", "sS", s_up)
                fuse_ct[(io, qb)] = kb.op("vector", lambda e, io=io, qb=qb: e.tensor_mul(
                    fuse_ap(io, qb), PB[:, 0:512], PB[:, 512:1024]), inc="sV")
                last_fu[0] = fuse_ct[(io, qb)]
        cut(11)
        dn_mm3 = {}
        dn_ev = {}
        dn_tct = {}
        dn_st = {}
        for dt in range(16):
            lds = {}
            for t3 in range(3):
                L = dt * 3 + t3
                slot = L % 4
                qd = "gpsimd"
                if L >= 4:
                    dtp, t3p = divmod(L - 4, 3)
                    kb.wait(qd, "sT", dn_mm3[(dtp, 1, t3p)])
                else:
                    kb.wait(qd, "sT", gu_mm[46 + (L % 2)])
                din_ap = A(downw, (t3 * 16) * 128 * 2048 + dt * 128,
                           [[2048, 128], [128 * 2048, 16], [1, 128]])
                lds[t3] = (slot, kb.dma(qd, lambda e, din_ap=din_ap, slot=slot: e.dma_start(
                    out=W8[:, slot * 2048:slot * 2048 + 2048], in_=din_ap)))
            for qb in range(2):
                gi = dt * 2 + qb
                pi = gi % 2
                pst = PS[4 + gi % 2]
                ate = F32A[:, 2048 + (gi % 2) * 512:2048 + (gi % 2) * 512 + 512]
                rslot = F32A[:, 3072:3584] if gi % 2 == 0 else F32A[:, 0:512]
                aout = F32A[:, 3584:4096] if gi % 2 == 0 else F32A[:, 1536:2048]
                prev = dn_ev.get(gi - 2)
                if prev:
                    kb.wait("tensor", "sV", prev)
                if dt == 0:
                    kb.wait("tensor", "sV", fuse_ct[(47, qb)])
                    kb.wait("tensor", "sS", sup_by[95])
                for t3 in range(3):
                    slot, ld3 = lds[t3]
                    kb.wait("tensor", *ld3)
                    for cc in range(16):
                        c = t3 * 16 + cc
                        dn_mm3[(dt, qb, t3)] = kb.op("tensor", lambda e, slot=slot, cc=cc, c=c, qb=qb,
                                                     pi=pi, isf=(t3 == 0 and cc == 0),
                                                     isl=(t3 == 2 and cc == 15): e.matmul(
                            PS[pi][:, :], W8[:, slot * 2048 + cc * 128:slot * 2048 + (cc + 1) * 128],
                            fuse_ap(c, qb), start=isf, stop=isl),
                            inc="sT" if cc == 15 else None)
                mmd = dn_mm3[(dt, qb, 2)]
                kb.wait("scalar", "sT", mmd)
                prev_tct = dn_tct.get(gi - 2)
                if prev_tct:
                    kb.wait("scalar", "sT", prev_tct)
                v_dn = kb.op("scalar", lambda e, pi=pi, dt=dt, ate=ate: e.activation(
                    ate, PS[pi][:, :], AFT.Identity, scale=DSCc[:, dt:dt + 1]), inc="sS")
                rin = A(res_d, (qb * 512) * 2048 + dt * 128, [[2048, 128], [128 * 2048, 4], [1, 128]])
                if gi < 2:
                    kb.wait("sync", "sV", ln2ct[7])
                prev_ev = dn_ev.get(gi - 2)
                if prev_ev:
                    kb.wait("sync", "sV", prev_ev)
                ld_blk = kb.dma("sync", lambda e, rin=rin, rslot=rslot: e.dma_start(
                    out=rslot, in_=rin))
                kb.wait("tensor", "sS", v_dn)
                if prev_ev:
                    kb.wait("tensor", "sV", prev_ev)
                for tq in range(4):
                    tct = kb.op("tensor", lambda e, tq=tq, ate=ate, pst=pst: e.matmul(
                        pst[:, tq * 128:(tq + 1) * 128], ate[:, tq * 128:(tq + 1) * 128], IDF[:, :],
                        is_transpose=True, start=(tq == 0), stop=(tq == 3)),
                        inc="sT" if tq == 3 else None)
                dn_tct[gi] = tct
                kb.wait("vector", "sT", tct)
                kb.wait("vector", *ld_blk)
                if (gi - 2) in dn_st:
                    kb.wait("vector", *dn_st[gi - 2])
                elif prev_store[0] and gi < 2:
                    kb.wait("vector", *prev_store[0])
                dn_ev[gi] = kb.op("vector", lambda e, pst=pst, aout=aout, rslot=rslot: e.tensor_add(
                    aout, pst[:, 0:512], rslot), inc="sV")
                kb.wait("sync", "sV", dn_ev[gi])
                oout = A(out_d, (qb * 512) * 2048 + dt * 128, [[2048, 128], [128 * 2048, 4], [1, 128]])
                dn_st[gi] = kb.dma("sync", lambda e, oout=oout, aout=aout: e.dma_start(
                    out=oout, in_=aout))

    except _Cut:
        pass
    kb.wait_all_dma("sync")
    kb.wait_all_dma("gpsimd")

    blk = es.enter_context(nc.Block())
    for name in ENGINES:
        def body(eng, name=name):
            for fn in kb.ops[name]:
                fn(eng)
        getattr(blk, name)(body)
    es.close()
    return nc


_NC = [None]


def kernel(L0_attn_x, L0_attn_freqs, L0_attn_wq_w, L0_attn_wq_b, L0_attn_wkv_a_w,
           L0_attn_wkv_b_w, L0_attn_norm_w, L0_attn_out_w, L0_attn_out_b,
           L0_ffn_gate_w, L0_ffn_up_w, L0_ffn_down_w, L0_ffn_gate_scale,
           L0_ffn_up_scale, L0_ffn_down_scale):
    f32 = lambda a: np.ascontiguousarray(np.asarray(a, dtype=np.float32))
    x = f32(L0_attn_x)
    cols = lambda v, n: np.ascontiguousarray(f32(v).reshape(n, 128).T)
    tr = lambda a: np.ascontiguousarray(f32(a).T)
    shared = {
        "wqT": tr(L0_attn_wq_w), "wkvaT": tr(L0_attn_wkv_a_w), "wkvbT": tr(L0_attn_wkv_b_w),
        "outwT": tr(L0_attn_out_w), "gatew": f32(L0_ffn_gate_w), "upw": f32(L0_ffn_up_w),
        "downw": f32(L0_ffn_down_w),
        "wqb_c": cols(L0_attn_wq_b, 16), "outb_c": cols(L0_attn_out_b, 16),
        "normw_c": cols(L0_attn_norm_w, 4), "gsc_c": cols(L0_ffn_gate_scale, 48),
        "usc_c": cols(L0_ffn_up_scale, 48), "dsc_c": cols(L0_ffn_down_scale, 16),
        "c2": np.ascontiguousarray(f32(L0_attn_freqs)[:16].reshape(16, 64).T),
        "m2": _m2_np(), "identb": np.eye(128).astype(bfd),
        "identf": np.eye(128, dtype=np.float32),
        "onesc": np.ones((128, 1), dtype=bfd),
        "onesr": np.ones((1, 128), dtype=np.float32),
        "masks": _masks_np(),
    }
    in_maps = []
    for c in range(8):
        b, h = c // 2, c % 2
        xbv = x[b] if h == 0 else np.ascontiguousarray(
            np.concatenate([x[b, 1024:], x[b, :1024]], axis=0))
        m = dict(shared)
        m["xb"] = xbv
        m["mbias"] = _mbias_np(h)
        in_maps.append(m)
    if _NC[0] is None:
        _NC[0] = build_nc()
    res = run_bass_kernel_spmd(_NC[0], in_maps, core_ids=list(range(8)))
    y = np.empty((4, 2048, 2048), dtype=np.float32)
    for c in range(8):
        b, h = c // 2, c % 2
        y[b, h * 1024:(h + 1) * 1024] = res.results[c]["out"]
    return y


def _m2_np():
    m2 = np.zeros((64, 128), dtype=np.float32)
    for i in range(32):
        for cc in range(2):
            for d in range(2):
                m2[2 * i + cc, 64 + 2 * i + d] = 1.0
    return m2


def _masks_np():
    i = np.arange(128)[:, None]
    j = np.arange(512)[None, :]
    ms = [(i + m * 128 <= j).astype(np.float32) for m in range(4)]
    return np.ascontiguousarray(np.concatenate(ms, axis=1).astype(bfd))


def _mbias_np(h):
    mb = np.zeros((2, 16), dtype=np.float32)
    if h == 0:
        mb[0, 4:16] = NEG
        mb[1, 8:16] = NEG
    flat = mb.reshape(32)
    return np.ascontiguousarray(np.repeat(flat[None, :], 128, axis=0))


# version 28
# speedup vs baseline: 1.0362x; 1.0104x over previous
import os
import numpy as np
import ml_dtypes
import contextlib
import concourse.bass as bass
import concourse.mybir as mybir
from concourse.bass_utils import run_bass_kernel_spmd
from concourse.alu_op_type import AluOpType as Op

F32 = mybir.dt.float32
BF16 = mybir.dt.bfloat16
FP8 = mybir.dt.float8e4
AFT = mybir.ActivationFunctionType
AX = mybir.AxisListType
bfd = ml_dtypes.bfloat16

D_, H_, DL = 2048, 16, 512
SS = 1.0 / float(np.sqrt(128.0))
NEG = -1.0e9
ENGINES = ("sync", "gpsimd", "vector", "scalar", "tensor")


class KB:
    def __init__(self, nc):
        self.nc = nc
        self.ops = {e: [] for e in ENGINES}
        self.cnt = {}
        self.semh = {}
        self.lastwait = {}
        self.pt_ev = [0, 0]
        self.dma_pools = {}
        self.dma_rr = {}

    def init_pools(self, es, queues=("sync", "gpsimd"), n=6):
        for q in queues:
            self.dma_pools[q] = []
            for i in range(n):
                name = f"d{q[0]}{i}"
                self.sem(name, es)
                self.dma_pools[q].append(name)
            self.dma_rr[q] = 0

    def dma(self, queue, fn):
        name = self.dma_pools[queue][self.dma_rr[queue]]
        self.dma_rr[queue] = (self.dma_rr[queue] + 1) % len(self.dma_pools[queue])
        self.wait(queue, name, self.cnt[name])
        v = self.op(queue, fn, inc=name, n=16)
        return (name, v)

    def wait_all_dma(self, eng):
        for q, pool in self.dma_pools.items():
            for name in pool:
                self.wait(eng, name, self.cnt[name])

    def sem(self, name, es):
        self.semh[name] = es.enter_context(self.nc.semaphore(name))
        self.cnt[name] = 0

    def op(self, eng, fn, inc=None, n=1):
        if inc is None:
            self.ops[eng].append(fn)
            return None
        self.cnt[inc] += n
        v = self.cnt[inc]
        h = self.semh[inc]

        def w(e, fn=fn, h=h, n=n):
            fn(e).then_inc(h, n)

        self.ops[eng].append(w)
        return v

    def wait(self, eng, name, v):
        if v is None or v <= 0:
            return
        key = (eng, name)
        if self.lastwait.get(key, 0) >= v:
            return
        self.lastwait[key] = v
        h = self.semh[name]
        self.ops[eng].append(lambda e, h=h, v=v: e.wait_ge(h, v))

    def barrier(self):
        snap = dict(self.cnt)
        for eng in ENGINES:
            for s, v in snap.items():
                self.wait(eng, s, v)


class _Cut(Exception):
    pass


def build_nc():
    PH = float(os.environ.get("KPH", "99"))

    def cut(n):
        if PH <= n:
            raise _Cut()

    nc = bass.Bass("TRN2", target_bir_lowering=False)

    def din(name, shape, dt=F32):
        return nc.dram_tensor(name, shape, dt, kind="ExternalInput")

    xb = din("xb", [2048, 2048])
    wqT = din("wqT", [2048, 2048])
    wkvaT = din("wkvaT", [2048, 512])
    wkvbT = din("wkvbT", [512, 4096])
    outwT = din("outwT", [2048, 2048])
    gatew = din("gatew", [2048, 6144])
    upw = din("upw", [2048, 6144])
    downw = din("downw", [6144, 2048])
    wqb_c = din("wqb_c", [128, 16])
    outb_c = din("outb_c", [128, 16])
    normw_c = din("normw_c", [128, 4])
    gsc_c = din("gsc_c", [128, 48])
    usc_c = din("usc_c", [128, 48])
    dsc_c = din("dsc_c", [128, 16])
    c2_d = din("c2", [64, 16])
    m2_d = din("m2", [64, 128])
    identb_d = din("identb", [128, 128], BF16)
    identf_d = din("identf", [128, 128], F32)
    onesc_d = din("onesc", [128, 1], BF16)
    onesr_d = din("onesr", [1, 128], F32)
    masks_d = din("masks", [128, 2048], BF16)
    mbias_d = din("mbias", [128, 32], F32)
    out_d = nc.dram_tensor("out", [1024, 2048], F32, kind="ExternalOutput")
    res_d = nc.dram_tensor("res_scr", [1024, 2048], F32, kind="Internal")
    lrow_d = nc.dram_tensor("lrow_scr", [32, 512], F32, kind="Internal")
    linv_d = nc.dram_tensor("linv_scr", [32, 512], F32, kind="Internal")

    es = contextlib.ExitStack()
    sb = lambda n, sh, dt: es.enter_context(nc.sbuf_tensor(n, sh, dt))
    pt = lambda n, sh, dt: es.enter_context(nc.psum_tensor(n, sh, dt))

    BIG0 = sb("BIG0", [128, 32768], BF16)
    BIG1 = sb("BIG1", [128, 16384], BF16)
    BIG2 = sb("BIG2", [128, 16384], BF16)
    LATT = sb("LATT", [128, 8192], BF16)
    XHB = sb("XHB", [128, 4096], BF16)
    WT = sb("WT", [128, 4096], BF16)
    PB = sb("PB", [128, 1024], BF16)
    SQB = sb("SQB", [128, 2048], BF16)
    F32A = sb("F32A", [128, 4096], F32)
    W8 = sb("W8", [128, 8192], FP8)
    MASKS = sb("MASKS", [128, 2048], BF16)
    WQBc = sb("WQBc", [128, 16], F32)
    OUTBc = sb("OUTBc", [128, 16], F32)
    NORMWc = sb("NORMWc", [128, 4], F32)
    GSCc = sb("GSCc", [128, 48], F32)
    USCc = sb("USCc", [128, 48], F32)
    DSCc = sb("DSCc", [128, 16], F32)
    C2s = sb("C2s", [64, 16], F32)
    M2s = sb("M2s", [64, 128], F32)
    IDB = sb("IDB", [128, 128], BF16)
    IDF = sb("IDF", [128, 128], F32)
    ONESCb = sb("ONESCb", [128, 1], BF16)
    ONESRf = sb("ONESRf", [1, 128], F32)
    MBIASs = sb("MBIASs", [128, 32], F32)
    WROPE = sb("WROPE", [128, 16], F32)
    WQBG = sb("WQBG", [128, 16], F32)
    RSTD = sb("RSTD", [128, 16], F32)
    SCL = sb("SCL", [128, 16], F32)
    SCLSQ = sb("SCLSQ", [128, 16], F32)
    BIASC = sb("BIASC", [128, 32], F32)
    BNS = sb("BNS", [128, 24], F32)
    AGG = sb("AGG", [128, 16], F32)
    STAT = sb("STAT", [128, 8], F32)
    KC = sb("KC", [128, 16], F32)
    MAXQ = sb("MAXQ", [128, 1], F32)
    MAXK = sb("MAXK", [128, 1], F32)
    ROWA = sb("ROWA", [1, 128], F32)
    MUx = sb("MUx", [1, 4], F32)
    MUCOL = sb("MUCOL", [128, 1], F32)
    LINV = sb("LINV", [1, 512], F32)
    EPSC = sb("EPSC", [128, 2], F32)
    SEL = sb("SEL", [32, 384], F32)
    X4 = sb("X4", [128, 2048], BF16)

    PS = [pt(f"PS{i}", [128, 512], F32) for i in range(6)]
    PT = [pt("PT0", [128, 1024], BF16), pt("PT1", [128, 1024], BF16)]

    kb = KB(nc)
    for s in ("sT", "sV", "sS", "sG"):
        kb.sem(s, es)
    kb.init_pools(es)
    A = bass.AP

    def tp_group(src, n, pti, dst, src_f32=False):
        kb.wait("tensor", "sV", kb.pt_ev[pti])
        for j in range(n):
            tct = kb.op("tensor", lambda e, j=j: e.matmul(
                PT[pti][:, j * 128:(j + 1) * 128], src[:, j * 128:(j + 1) * 128], IDB[:, :],
                is_transpose=True, start=(j == 0), stop=(j == n - 1)),
                inc="sT" if j == n - 1 else None)
        kb.wait("vector", "sT", tct)
        out_ap, in_ap = dst(n)
        ev = kb.op("vector", lambda e, o=out_ap, i=in_ap: e.tensor_copy(o, i), inc="sV")
        kb.pt_ev[pti] = ev
        return ev

    def strided_dst(base, off, n, chunk_stride, width_elems):
        out_ap = A(base, off, [[width_elems, 128], [chunk_stride, n], [1, 128]])
        return out_ap

    def load_strip(src_ap, buf, n_t, wait_mm=None):
        if wait_mm:
            kb.wait("gpsimd", "sT", wait_mm)
        return kb.dma("gpsimd", lambda e, s=src_ap, buf=buf, n_t=n_t: e.dma_start(
            out=WT[:, buf * 2048:buf * 2048 + n_t * 128], in_=s))

    try:
        XT = [F32A[:, 0:2048], F32A[:, 2048:4096]]
        consts = [(IDB, identb_d), (M2s, m2_d), (C2s, c2_d), (WQBc, wqb_c),
                  (IDF, identf_d), (ONESCb, onesc_d), (ONESRf, onesr_d),
                  (MASKS, masks_d), (MBIASs, mbias_d),
                  (OUTBc, outb_c), (NORMWc, normw_c), (GSCc, gsc_c),
                  (USCc, usc_c), (DSCc, dsc_c)]
        cld = {}
        for dst0, src0 in consts[:4]:
            cld[dst0.name] = kb.dma("sync", lambda e, d=dst0, s=src0: e.dma_start(
                out=d[:, :], in_=s.ap()[:, :]))
        preA = {it: kb.dma("sync" if it == 0 else "gpsimd", lambda e, it=it: e.dma_start(
            out=XT[it % 2], in_=xb.ap()[it * 128:(it + 1) * 128, :])) for it in (0, 1)}
        for dst0, src0 in consts[4:]:
            cld[dst0.name] = kb.dma("sync", lambda e, d=dst0, s=src0: e.dma_start(
                out=d[:, :], in_=s.ap()[:, :]))
        kb.op("vector", lambda e: e.memset(EPSC[:, 0:1], 1.0e-5))
        kb.op("vector", lambda e: e.memset(EPSC[:, 1:2], 1.0e-6))
        kb.op("gpsimd", lambda e: e.iota(
            SEL[0:32, 256:384], [[0, 128]], base=0, channel_multiplier=1,
            allow_small_or_imprecise_dtypes=True), inc="sG")
        kb.wait("vector", *cld["C2s"])
        kb.wait("vector", *cld["WQBc"])
        v_sq = kb.op("vector", lambda e: e.tensor_mul(AGG[0:64, 0:16], C2s[0:64, :], C2s[0:64, :]), inc="sV")
        kb.wait("tensor", "sV", v_sq)
        kb.wait("tensor", *cld["M2s"])
        kb.wait("tensor", *cld["IDB"])
        t_g = kb.op("tensor", lambda e: e.matmul(PS[5][:, 0:16], M2s[0:64, :], AGG[0:64, 0:16],
                                                 start=True, stop=True), inc="sT")
        kb.op("vector", lambda e: e.memset(WROPE[0:64, 0:16], 1.0))
        kb.wait("vector", "sT", t_g)
        vw = kb.op("vector", lambda e: e.tensor_copy(WROPE[64:128, 0:16], PS[5][64:128, 0:16]), inc="sV")
        kb.wait("vector", "sV", vw)
        kb.op("vector", lambda e: e.tensor_mul(WQBG[:, 0:16], WROPE[:, 0:16], WQBc[:, 0:16]))

        cut(1)
        strips_B = {dl: load_strip(A(wkvaT, dl * 128, [[512, 128], [128 * 512, 16], [1, 128]]),
                                   dl % 2, 16) for dl in (0, 1)}
        def vedge(fn):
            v = kb.op("vector", fn, inc="sV")
            kb.wait("vector", "sV", v)
            return v

        def warmT(upto, cap=5):
            lo = max(kb.lastwait.get(("tensor", "sV"), 0) + 1, upto - cap)
            for v in range(lo, upto):
                kb.wait("tensor", "sV", v)
                kb.op("tensor", lambda e: e.matmul(PS[5][0:1, 0:1], IDB[:, 0:1], IDB[:, 0:1],
                                                   start=True, stop=True))

        def ln_tile(buf, xh, eps_col):
            kb.op("scalar", lambda e, buf=buf: e.activation(
                BIG2[:, 0:2048], buf, AFT.Identity, accum_out=STAT[:, 0:1]), inc="sS")
            s_sq = kb.op("scalar", lambda e, buf=buf: e.activation(
                SQB[:, 0:2048], buf, AFT.Square, accum_out=STAT[:, 1:2]), inc="sS")
            kb.wait("vector", "sS", s_sq)
            vedge(lambda e: e.tensor_scalar(STAT[:, 2:3], STAT[:, 0:1], 1.0 / 2048, None, Op.mult))
            vedge(lambda e: e.tensor_mul(STAT[:, 3:4], STAT[:, 2:3], STAT[:, 2:3]))
            v_var = kb.op("vector", lambda e: e.scalar_tensor_tensor(
                STAT[:, 5:6], STAT[:, 1:2], 1.0 / 2048, STAT[:, 3:4],
                Op.mult, Op.subtract), inc="sV")
            kb.wait("vector", "sV", v_var)
            kb.wait("scalar", "sV", v_var)
            s_sd = kb.op("scalar", lambda e, eps_col=eps_col: e.activation(
                STAT[:, 6:7], STAT[:, 5:6], AFT.Sqrt, bias=EPSC[:, eps_col:eps_col + 1],
                scale=1.0), inc="sS")
            kb.wait("vector", "sS", s_sd)
            vedge(lambda e: e.reciprocal(STAT[:, 7:8], STAT[:, 6:7]))
            return kb.op("vector", lambda e, xh=xh, buf=buf: e.tensor_scalar(
                xh, buf, STAT[:, 2:3], STAT[:, 7:8], Op.subtract, Op.mult), inc="sV")

        lnct, evA = {}, {}
        for it in range(16):
            if it < 2:
                ld = preA[it]
            else:
                qx = "sync" if it % 2 == 0 else "gpsimd"
                kb.wait(qx, "sV", lnct[it - 2])
                ld = kb.dma(qx, lambda e, it=it: e.dma_start(
                    out=XT[it % 2], in_=xb.ap()[it * 128:(it + 1) * 128, :]))
            kb.wait("vector", *ld)
            kb.wait("scalar", *ld)
            xh = XHB[:, (it % 2) * 2048:(it % 2) * 2048 + 2048]
            if it >= 2:
                kb.wait("vector", "sT", evA[it - 2])
            lnct[it] = ln_tile(XT[it % 2], xh, 0)
            warmT(lnct[it])
            kb.wait("tensor", "sV", lnct[it])
            for half in range(2):
                src = xh[:, half * 1024:half * 1024 + 1024]
                dst = lambda n, half=half, it=it, pti=half: (
                    A(BIG0, half * 8 * 2048 + it * 128, [[32768, 128], [2048, 8], [1, 128]]),
                    PT[half][:, 0:1024])
                evA[it] = tp_group(src, 8, half, dst)
            evA[it] = kb.cnt["sT"]
        kb.barrier()

        cut(2)
        first_col = [True]
        lat_ev = {}
        mm_lat = {}
        col_prev = None
        pending = None

        def emit_cols(idx, tb, sqct, is_last):
            kb.wait("tensor", "sS", sqct)
            slot = idx % 2
            cp = None
            for tt in range(4):
                isf = first_col[0]
                first_col[0] = False
                isl = (is_last and tt == 3)
                cp = kb.op("tensor", lambda e, tt=tt, tb=tb, slot=slot, isf=isf, isl=isl: e.matmul(
                    PS[4][:, tb * 4 + tt:tb * 4 + tt + 1],
                    SQB[:, slot * 512 + tt * 128:slot * 512 + (tt + 1) * 128], ONESCb[:, 0:1],
                    start=isf, stop=isl), inc="sT")
            return cp

        for dl in range(4):
            if dl < 2:
                strip = strips_B[dl]
            else:
                strip = load_strip(A(wkvaT, dl * 128, [[512, 128], [128 * 512, 16], [1, 128]]),
                                   dl % 2, 16, wait_mm=mm_lat.get(dl - 2))
            if dl == 0:
                cut(2.2)
            kb.wait("tensor", *strip)
            for tb in range(4):
                idx = dl * 4 + tb
                pi = idx % 4
                prev = lat_ev.get(idx - 4)
                if prev:
                    kb.wait("tensor", "sS", prev)
                for c in range(16):
                    mmct = kb.op("tensor", lambda e, c=c, pi=pi, dl=dl, tb=tb: e.matmul(
                        PS[pi][:, :], WT[:, (dl % 2) * 2048 + c * 128:(dl % 2) * 2048 + (c + 1) * 128],
                        BIG0[:, c * 2048 + tb * 512:c * 2048 + tb * 512 + 512],
                        start=(c == 0), stop=(c == 15)), inc="sT" if c == 15 else None)
                mm_lat[dl] = mmct
                if dl == 0 and tb == 0:
                    cut(2.3)
                kb.wait("scalar", "sT", mmct)
                if col_prev:
                    kb.wait("scalar", "sT", col_prev)
                v_sq2 = kb.op("scalar", lambda e, pi=pi, idx=idx: e.activation(
                    SQB[:, (idx % 2) * 512:(idx % 2) * 512 + 512], PS[pi][:, :],
                    AFT.Square), inc="sS")
                if dl == 0 and tb == 0:
                    cut(2.45)
                lat_ev[idx] = kb.op("scalar", lambda e, pi=pi, dl=dl, tb=tb: e.activation(
                    LATT[:, dl * 2048 + tb * 512:dl * 2048 + tb * 512 + 512],
                    PS[pi][:, :], AFT.Identity, scale=NORMWc[:, dl:dl + 1]), inc="sS")
                if dl == 0 and tb == 0:
                    cut(2.6)
                if pending is not None:
                    col_prev = emit_cols(*pending, is_last=False)
                pending = (idx, tb, v_sq2)
        strips_C = {ot: load_strip(A(wqT, ot * 128, [[2048, 128], [128 * 2048, 16], [1, 128]]),
                                   ot % 2, 16, wait_mm=mm_lat[2 + ot]) for ot in (0, 1)}
        col_prev = emit_cols(*pending, is_last=True)
        kb.wait("scalar", "sT", col_prev)
        s_sr = kb.op("scalar", lambda e: e.activation(RSTD[:, 0:16], PS[4][:, 0:16], AFT.Sqrt,
                                                      bias=EPSC[:, 1:2], scale=1.0 / 512), inc="sS")
        kb.wait("vector", "sS", s_sr)
        vedge(lambda e: e.reciprocal(RSTD[:, 0:16], RSTD[:, 0:16]))
        vedge(lambda e: e.tensor_scalar(SCL[:, 0:16], RSTD[:, 0:16], SS, None, Op.mult))
        vedge(lambda e: e.tensor_mul(SCLSQ[:, 0:16], SCL[:, 0:16], SCL[:, 0:16]))
        kb.barrier()

        cut(3)
        q_ev = {}
        mm_q = {}
        qcol_by = {}
        qn_red = {}
        v_mq_box = [None]

        def emit_qnorm(h):
            sqb_h = SQB if h % 2 == 0 else X4
            kb.wait("vector", "sS", q_ev[h * 2 + 1])
            if h >= 2:
                kb.wait("vector", "sT", qcol_by[h - 2])
            v_qsq = kb.op("vector", lambda e, h=h, sqb_h=sqb_h: e.tensor_mul(
                sqb_h[:, 0:1024], BIG1[:, h * 1024:(h + 1) * 1024],
                BIG1[:, h * 1024:(h + 1) * 1024]), inc="sV")
            kb.wait("tensor", "sV", v_qsq)
            if h >= 1:
                kb.wait("tensor", "sV", qn_red[h - 1])
            for tq in range(8):
                qcol = kb.op("tensor", lambda e, tq=tq, sqb_h=sqb_h: e.matmul(
                    PS[4][:, tq:tq + 1], sqb_h[:, tq * 128:(tq + 1) * 128], ONESCb[:, 0:1],
                    start=(tq == 0), stop=(tq == 7)), inc="sT" if tq == 7 else None)
            qcol_by[h] = qcol
            kb.wait("vector", "sT", qcol)
            qn_red[h] = vedge(lambda e: e.tensor_reduce(STAT[:, 0:1], PS[4][:, 0:8], AX.X, Op.max))
            if h == 0:
                v_mq_box[0] = vedge(lambda e: e.tensor_copy(MAXQ[:, 0:1], STAT[:, 0:1]))
            else:
                v_mq_box[0] = vedge(lambda e: e.tensor_max(MAXQ[:, 0:1], MAXQ[:, 0:1], STAT[:, 0:1]))

        for ot in range(16):
            if ot < 2:
                strip = strips_C[ot]
            else:
                strip = load_strip(A(wqT, ot * 128, [[2048, 128], [128 * 2048, 16], [1, 128]]),
                                   ot % 2, 16, wait_mm=mm_q.get(ot - 2))
            kb.wait("tensor", *strip)
            for qb in range(2):
                pi = (ot * 2 + qb) % 4
                prev = q_ev.get(ot * 2 + qb - 4)
                if prev:
                    kb.wait("tensor", "sS", prev)
                for c in range(16):
                    mmct = kb.op("tensor", lambda e, c=c, pi=pi, ot=ot, qb=qb: e.matmul(
                        PS[pi][:, :], WT[:, (ot % 2) * 2048 + c * 128:(ot % 2) * 2048 + (c + 1) * 128],
                        BIG0[:, c * 2048 + qb * 512:c * 2048 + qb * 512 + 512],
                        start=(c == 0), stop=(c == 15)), inc="sT" if c == 15 else None)
                mm_q[ot] = mmct
                kb.wait("scalar", "sT", mmct)
                q_ev[ot * 2 + qb] = kb.op("scalar", lambda e, pi=pi, ot=ot, qb=qb: e.activation(
                    BIG1[:, ot * 1024 + qb * 512:ot * 1024 + qb * 512 + 512],
                    PS[pi][:, :], AFT.Identity, bias=WQBG[:, ot:ot + 1],
                    scale=WROPE[:, ot:ot + 1]), inc="sS")
            if ot >= 1:
                emit_qnorm(ot - 1)
        emit_qnorm(15)
        v_mq = v_mq_box[0]
        kb.wait("gpsimd", "sV", v_mq)
        g_mq = kb.dma("gpsimd", lambda e: e.dma_start(out=ROWA[0:1, 0:128], in_=MAXQ[0:128, 0:1],
                                                      allow_slow_non_contiguous=True))
        kb.wait("vector", *g_mq)
        kb.op("vector", lambda e: e.tensor_reduce(MUx[0:1, 0:1], ROWA[0:1, 0:128], AX.X, Op.max))
        kb.barrier()

        cut(4)
        KOFF, VOFF = 0, 16384
        slot_lists = {0: [0, 1, 2, 3, 8, 9, 10, 11, 12, 13, 14, 15], 1: list(range(16))}
        pair_idx = 0
        bcast_rd = {}
        wr_by = {}
        PSL = [PB[:, 0:512], PB[:, 512:1024], SQB[:, 0:512], SQB[:, 512:1024],
               X4[:, 0:512], X4[:, 512:1024], X4[:, 1024:1536], X4[:, 1536:2048]]
        PADD = [SQB[:, 1024:1536], SQB[:, 1536:2048]]
        linv_by_pair = {}
        otev_by_pair = {}
        exp_ring = {}
        av_ring = {}
        av_by = {}
        padd_ring = {}
        g_ctr = [0]
        bcast_S = [None]

        for hg in range(2):
            k_ev = {}
            mm_k = {}
            vksq_by = {}
            kcol_k = {}
            kn_red = {}
            v_mk_box = [None]
            pend_kn = [None]

            def emit_kn(i8):
                sqb_i = SQB if i8 % 2 == 0 else X4
                kb.wait("tensor", "sV", vksq_by[i8])
                if i8 >= 1:
                    kb.wait("tensor", "sV", kn_red[i8 - 1])
                for kt in range(16):
                    kcol = kb.op("tensor", lambda e, kt=kt, sqb_i=sqb_i: e.matmul(
                        PS[4][:, kt:kt + 1], sqb_i[:, kt * 128:(kt + 1) * 128], ONESCb[:, 0:1],
                        start=(kt == 0), stop=(kt == 15)), inc="sT" if kt == 15 else None)
                kcol_k[i8] = kcol
                kb.wait("vector", "sT", kcol)
                kn_red[i8] = vedge(lambda e: e.tensor_mul(KC[:, 0:16], PS[4][:, 0:16], SCLSQ[:, 0:16]))
                vedge(lambda e: e.tensor_reduce(STAT[:, 0:1], KC[:, 0:16], AX.X, Op.max))
                if i8 == 0:
                    v_mk_box[0] = vedge(lambda e: e.tensor_copy(MAXK[:, 0:1], STAT[:, 0:1]))
                else:
                    v_mk_box[0] = vedge(lambda e: e.tensor_max(MAXK[:, 0:1], MAXK[:, 0:1], STAT[:, 0:1]))

            for i8 in range(8):
                h = hg * 8 + i8
                strip = load_strip(A(wkvbT, h * 128, [[4096, 128], [128 * 4096, 4], [1, 128]]),
                                   i8 % 2, 4, wait_mm=mm_k.get(i8 - 2))
                kb.wait("tensor", *strip)
                prev = k_ev.get(i8 - 1)
                if prev:
                    kb.wait("tensor", "sV", prev)
                for tb in range(4):
                    for c in range(4):
                        mmct = kb.op("tensor", lambda e, c=c, tb=tb, i8=i8: e.matmul(
                            PS[tb][:, :], WT[:, (i8 % 2) * 2048 + c * 128:(i8 % 2) * 2048 + (c + 1) * 128],
                            LATT[:, c * 2048 + tb * 512:c * 2048 + tb * 512 + 512],
                            start=(c == 0), stop=(c == 3)),
                            inc="sT" if (tb == 3 and c == 3) else None)
                mm_k[i8] = mmct
                kb.wait("vector", "sT", mmct)
                for tb in range(4):
                    k_ev[i8] = kb.op("vector", lambda e, tb=tb, i8=i8: e.tensor_copy(
                        BIG0[:, KOFF + i8 * 2048 + tb * 512:KOFF + i8 * 2048 + tb * 512 + 512],
                        PS[tb][:, :]), inc="sV")
                kb.wait("vector", "sV", k_ev[i8])
                if i8 >= 2:
                    kb.wait("vector", "sT", kcol_k[i8 - 2])
                vksq_by[i8] = kb.op("vector", lambda e, i8=i8, sqb_i=(SQB if i8 % 2 == 0 else X4): e.tensor_mul(
                    sqb_i[:, 0:2048], BIG0[:, KOFF + i8 * 2048:KOFF + (i8 + 1) * 2048],
                    BIG0[:, KOFF + i8 * 2048:KOFF + (i8 + 1) * 2048]), inc="sV")
                if pend_kn[0] is not None:
                    emit_kn(pend_kn[0])
                pend_kn[0] = i8
            emit_kn(7)
            v_mk = v_mk_box[0]
            cut(5 - hg)
            vin = A(wkvbT, 2048 + hg * 1024, [[4096, 128], [128 * 4096, 4], [1, 1024]])
            wv_ld = kb.dma("gpsimd", lambda e, vin=vin: e.dma_start(
                out=A(XHB, 0, [[4096, 128], [1024, 4], [1, 1024]]), in_=vin))
            kb.wait("gpsimd", "sV", v_mk)
            g_mk = kb.dma("gpsimd", lambda e: e.dma_start(out=ROWA[0:1, 0:128], in_=MAXK[0:128, 0:1],
                                                          allow_slow_non_contiguous=True))
            kb.wait("vector", *g_mk)
            vedge(lambda e: e.tensor_reduce(MUx[0:1, 1:2], ROWA[0:1, 0:128], AX.X, Op.max))
            v_mu2 = vedge(lambda e: e.tensor_mul(MUx[0:1, 2:3], MUx[0:1, 0:1], MUx[0:1, 1:2]))
            kb.wait("scalar", "sV", v_mu2)
            s_mu = kb.op("scalar", lambda e: e.activation(MUx[0:1, 3:4], MUx[0:1, 2:3],
                                                          AFT.Sqrt), inc="sS")
            kb.wait("vector", "sS", s_mu)
            v_mun = kb.op("vector", lambda e: e.tensor_scalar(MUx[0:1, 3:4], MUx[0:1, 3:4],
                                                              -1.0, None, Op.mult), inc="sV")
            kb.wait("tensor", "sV", v_mun)
            t_muc = kb.op("tensor", lambda e: e.matmul(PS[5][:, 0:1], ONESRf[0:1, :], MUx[0:1, 3:4],
                                                       start=True, stop=True), inc="sT")
            kb.wait("vector", "sT", t_muc)
            vedge(lambda e: e.tensor_copy(MUCOL[:, 0:1], PS[5][:, 0:1]))
            v_bias = vedge(lambda e: e.tensor_scalar(BIASC[:, 0:32], MBIASs[:, 0:32],
                                                     MUCOL[:, 0:1], None, Op.add))
            cut(7 - hg)
            kb.wait("tensor", *wv_ld)
            kb.wait("tensor", "sV", k_ev[7])
            v_ev = {}
            vts_by = {}
            for tt in range(16):
                pb0 = (tt % 2) * 2
                if tt >= 2:
                    kb.wait("tensor", "sS", v_ev[(tt - 2, 1)])
                    kb.wait("tensor", "sV", vts_by[tt - 2])
                for dl in range(4):
                    for ovb in range(2):
                        mmv = kb.op("tensor", lambda e, tt=tt, dl=dl, ovb=ovb, pb0=pb0: e.matmul(
                            PS[pb0 + ovb][:, :], LATT[:, dl * 2048 + tt * 128:dl * 2048 + tt * 128 + 128],
                            XHB[:, dl * 1024 + ovb * 512:dl * 1024 + ovb * 512 + 512],
                            start=(dl == 0), stop=(dl == 3)),
                            inc="sT" if (dl == 3 and ovb == 1) else None)
                if tt == 0:
                    cut(5.6 - hg)
                kb.wait("vector", "sT", mmv)
                if tt >= 1:
                    kb.wait("vector", "sV", vts_by[tt - 1])
                vc = kb.op("vector", lambda e, pb0=pb0: e.tensor_copy(
                    F32A[:, 0:512], PS[pb0][:, :]), inc="sV")
                kb.wait("vector", "sV", vc)
                vts_by[tt] = kb.op("vector", lambda e, tt=tt: e.tensor_scalar(
                    BIG0[:, VOFF + tt * 1024:VOFF + tt * 1024 + 512],
                    F32A[:, 0:512], RSTD[:, tt:tt + 1], None, Op.mult), inc="sV")
                kb.wait("scalar", "sT", mmv)
                v_ev[(tt, 1)] = kb.op("scalar", lambda e, tt=tt, pb0=pb0: e.activation(
                    BIG0[:, VOFF + tt * 1024 + 512:VOFF + tt * 1024 + 1024],
                    PS[pb0 + 1][:, :], AFT.Identity, scale=RSTD[:, tt:tt + 1]), inc="sS")
            kb.barrier()
            cut(6 - hg)
            for i8 in range(8):
                h = hg * 8 + i8
                for qb in range(2):
                    slots = slot_lists[qb]
                    nsl = len(slots)
                    npair = nsl // 2
                    po = 2 + (pair_idx % 2)
                    gstart = g_ctr[0]
                    rdy = {}
                    padd_by = {}
                    lmm_ct = {}

                    def emit_avs(p, i8=i8, qb=qb, po=po, gstart=gstart, nsl=None):
                        kb.wait("tensor", *rdy[2 * p])
                        kb.wait("tensor", *rdy[2 * p + 1])
                        if p == 0 and pair_idx > 1:
                            kb.wait("tensor", "sV", otev_by_pair[pair_idx - 2])
                        for i in (2 * p, 2 * p + 1):
                            kt_i = slot_lists[qb][i]
                            pi_i = (gstart + i) % 8
                            av_by[(pair_idx, i)] = av_ring[gstart + i] = kb.op(
                                "tensor", lambda e, pi_i=pi_i, kt_i=kt_i, i8=i8,
                                isf=(i == 0), isl=(i == nsl - 1), po=po: e.matmul(
                                    PS[po][:, :],
                                    BIG0[:, VOFF + kt_i * 1024 + i8 * 128:VOFF + kt_i * 1024 + i8 * 128 + 128],
                                    PSL[pi_i],
                                    start=isf, stop=isl), inc="sT")

                    def emit_lmm(p, npair=None):
                        kb.wait("tensor", "sV", padd_by[p])
                        if p == 0 and pair_idx > 0:
                            kb.wait("tensor", "sV", linv_by_pair[pair_idx - 1])
                        lmm_ct[p] = kb.op("tensor", lambda e, p=p, isf=(p == 0),
                                          isl=(p == npair - 1): e.matmul(
                            PS[5][0:1, 0:512], ONESCb[:, 0:1], PADD[p % 2],
                            start=isf, stop=isl), inc="sT")

                    for j, kt in enumerate(slots):
                        g = gstart + j
                        pi = g % 8
                        sb_ = [PS[0], PS[1], PS[4]][g % 3]
                        if g >= 3:
                            kb.wait("tensor", "sS", exp_ring[g - 3])
                        t_s = kb.op("tensor", lambda e, sb_=sb_, i8=i8, kt=kt, qb=qb, h=h: e.matmul(
                            sb_[:, :],
                            BIG0[:, KOFF + i8 * 2048 + kt * 128:KOFF + i8 * 2048 + kt * 128 + 128],
                            BIG1[:, h * 1024 + qb * 512:h * 1024 + qb * 512 + 512],
                            start=True, stop=True), inc="sT")
                        kb.wait("scalar", "sT", t_s)
                        if g >= 8:
                            kb.wait("scalar", "sT", av_ring[g - 8])
                            kb.wait("scalar", "sV", padd_ring[g - 8])
                        exp_ring[g] = kb.op("scalar", lambda e, pi=pi, sb_=sb_, qb=qb, kt=kt: e.activation(
                            PSL[pi], sb_[:, :], AFT.Exp,
                            bias=BIASC[:, qb * 16 + kt:qb * 16 + kt + 1],
                            scale=SCL[:, kt:kt + 1]), inc="sS")
                        if 4 * qb <= kt <= 4 * qb + 3:
                            kb.wait("vector", "sS", exp_ring[g])
                            m = kt - 4 * qb
                            mk = kb.op("vector", lambda e, pi=pi, m=m: e.tensor_mul(
                                PSL[pi], PSL[pi], MASKS[:, m * 512:(m + 1) * 512]), inc="sV")
                            rdy[j] = ("sV", mk)
                        else:
                            rdy[j] = ("sS", exp_ring[g])
                        if j % 2 == 1:
                            p = j // 2
                            if p >= 1:
                                emit_avs(p - 1, nsl=nsl)
                            if p >= 2:
                                emit_lmm(p - 2, npair=npair)
                            kb.wait("vector", *rdy[j - 1])
                            kb.wait("vector", *rdy[j])
                            if p >= 2:
                                kb.wait("vector", "sT", lmm_ct[p - 2])
                            padd_by[p] = kb.op("vector", lambda e, p=p, gstart=gstart: e.tensor_add(
                                PADD[p % 2], PSL[(gstart + 2 * p) % 8],
                                PSL[(gstart + 2 * p + 1) % 8]), inc="sV")
                            padd_ring[gstart + 2 * p] = padd_by[p]
                            padd_ring[gstart + 2 * p + 1] = padd_by[p]
                    emit_avs(npair - 1, nsl=nsl)
                    emit_lmm(npair - 2, npair=npair)
                    emit_lmm(npair - 1, npair=npair)
                    g_ctr[0] += nsl
                    pgl = pair_idx % 16
                    kb.wait("vector", "sT", lmm_ct[npair - 1])
                    if (pair_idx - 1) in wr_by:
                        kb.wait("vector", *wr_by[pair_idx - 1])
                    linv_by_pair[pair_idx] = kb.op("vector", lambda e: e.tensor_copy(
                        LINV[0:1, 0:512], PS[5][0:1, 0:512]), inc="sV")
                    kb.wait("gpsimd", "sV", linv_by_pair[pair_idx])
                    wr_by[pair_idx] = kb.dma("gpsimd", lambda e, row=pair_idx: e.dma_start(
                        out=F32A[row:row + 1, 3072:3584], in_=LINV[0:1, 0:512],
                        allow_slow_non_contiguous=True))
                    otev_by_pair[pair_idx] = kb.op("vector", lambda e, po=po, h=h, qb=qb: e.tensor_copy(
                        BIG2[:, h * 1024 + qb * 512:h * 1024 + qb * 512 + 512],
                        PS[po][:, :]), inc="sV")
                    pair_idx += 1

            kb.barrier()

        cut(8)
        v_r32 = vedge(lambda e: e.reciprocal(
            F32A[0:32, 2560:3072], F32A[0:32, 3072:3584]))
        kb.wait("tensor", "sV", v_r32)
        last_n = None
        nbcs = {}
        mm_by = {}
        order = list(range(0, 32, 2)) + list(range(1, 32, 2))
        for k, p in enumerate(order):
            hq, qbp = p // 2, p % 2
            s = k % 2
            if k >= 2:
                kb.wait("vector", "sT", mm_by[order[k - 2]])
            v_sel = kb.op("vector", lambda e, p=p, s=s: e.tensor_scalar(
                SEL[0:32, s * 128:(s + 1) * 128], SEL[0:32, 256:384],
                float(p), None, Op.is_equal), inc="sV")
            kb.wait("tensor", "sV", v_sel)
            if k >= 2:
                kb.wait("tensor", "sV", nbcs[order[k - 2]])
            mm_by[p] = mmb = kb.op("tensor", lambda e, s=s: e.matmul(
                PS[2 + s][:, 0:512], SEL[0:32, s * 128:(s + 1) * 128],
                F32A[0:32, 2560:3072], start=True, stop=True), inc="sT")
            kb.wait("vector", "sT", mmb)
            last_n = kb.op("vector", lambda e, hq=hq, qbp=qbp, s=s: e.tensor_mul(
                BIG2[:, hq * 1024 + qbp * 512:hq * 1024 + qbp * 512 + 512],
                PS[2 + s][:, 0:512],
                BIG2[:, hq * 1024 + qbp * 512:hq * 1024 + qbp * 512 + 512]), inc="sV")
            nbcs[p] = last_n
        add_ct = {}
        mm_f = {}
        tctF = {}
        store_F = {}
        prev_store = [None]
        pend = [None]

        def flush_F(nxt_mm=None):
            pi, dt, qb, s_ev, ld_blk, xsl = pend[0]
            gi = dt * 2 + qb
            pst = PS[4 + gi % 2]
            aout = F32A[:, 3584:4096] if gi % 2 == 0 else F32A[:, 1024:1536]
            kb.wait("tensor", "sS", s_ev)
            pa = add_ct.get(gi - 2)
            if pa:
                kb.wait("tensor", "sV", pa[1])
            ate = F32A[:, 2048 + (gi % 2) * 512:2048 + (gi % 2) * 512 + 512]
            for tq in range(4):
                tct = kb.op("tensor", lambda e, tq=tq, ate=ate, pst=pst: e.matmul(
                    pst[:, tq * 128:(tq + 1) * 128], ate[:, tq * 128:(tq + 1) * 128], IDF[:, :],
                    is_transpose=True, start=(tq == 0), stop=(tq == 3)),
                    inc="sT" if tq == 3 else None)
            tctF[gi] = tct
            kb.wait("vector", "sT", tct)
            kb.wait("vector", *ld_blk)
            if (gi - 2) in store_F:
                kb.wait("vector", *store_F[gi - 2])
            v_add = kb.op("vector", lambda e, xsl=xsl, pst=pst, aout=aout: e.tensor_add(
                aout, pst[:, 0:512], F32A[:, xsl * 512:xsl * 512 + 512]), inc="sV")
            add_ct[gi] = (s_ev, v_add)
            kb.wait("sync", "sV", v_add)
            rout = A(res_d, (qb * 512) * 2048 + dt * 128, [[2048, 128], [128 * 2048, 4], [1, 128]])
            store_F[gi] = kb.dma("sync", lambda e, rout=rout, aout=aout: e.dma_start(
                out=rout, in_=aout))
            prev_store[0] = store_F[gi]
            pend[0] = None

        for dt in range(16):
            strip = load_strip(A(outwT, dt * 128, [[2048, 128], [128 * 2048, 16], [1, 128]]),
                               dt % 2, 16, wait_mm=mm_f.get(dt - 2))
            kb.wait("tensor", *strip)
            for qb in range(2):
                pi = (dt * 2 + qb) % 2
                prev = add_ct.get(dt * 2 + qb - 2)
                if prev:
                    kb.wait("tensor", "sS", prev[0])
                for c in range(16):
                    if dt == 0:
                        kb.wait("tensor", "sV", nbcs[c * 2 + qb])
                    mmct = kb.op("tensor", lambda e, c=c, pi=pi, dt=dt, qb=qb: e.matmul(
                        PS[pi][:, :], WT[:, (dt % 2) * 2048 + c * 128:(dt % 2) * 2048 + (c + 1) * 128],
                        BIG2[:, c * 1024 + qb * 512:c * 1024 + qb * 512 + 512],
                        start=(c == 0), stop=(c == 15)), inc="sT" if c == 15 else None)
                mm_f[dt] = mmct
                kb.wait("scalar", "sT", mmct)
                prev_tct = tctF.get(dt * 2 + qb - 1)
                if prev_tct:
                    kb.wait("scalar", "sT", prev_tct)
                ate = F32A[:, 2048 + pi * 512:2048 + pi * 512 + 512]
                s_ev = kb.op("scalar", lambda e, pi=pi, dt=dt, ate=ate: e.activation(
                    ate, PS[pi][:, :], AFT.Identity, bias=OUTBc[:, dt:dt + 1], scale=1.0), inc="sS")
                xin = A(xb, (qb * 512) * 2048 + dt * 128, [[2048, 128], [128 * 2048, 4], [1, 128]])
                xsl = (dt * 2 + qb) % 2
                prev_add = add_ct.get(dt * 2 + qb - 2)
                if prev_add:
                    kb.wait("sync", "sV", prev_add[1])
                ld_blk = kb.dma("sync", lambda e, xin=xin, xsl=xsl: e.dma_start(
                    out=F32A[:, xsl * 512:xsl * 512 + 512], in_=xin))
                if pend[0] is not None:
                    flush_F()
                pend[0] = (pi, dt, qb, s_ev, ld_blk, xsl)
        flush_F()
        kb.barrier()

        cut(9)
        kb.wait_all_dma("sync")
        kb.wait_all_dma("gpsimd")
        ln2ct, evG, vG = {}, {}, {}
        for it in range(8):
            qx = "sync" if it % 2 == 0 else "gpsimd"
            if it >= 2:
                kb.wait(qx, "sV", ln2ct[it - 2])
            ld = kb.dma(qx, lambda e, it=it: e.dma_start(
                out=XT[it % 2], in_=res_d.ap()[it * 128:(it + 1) * 128, :]))
            kb.wait("vector", *ld)
            kb.wait("scalar", *ld)
            xh = XHB[:, (it % 2) * 2048:(it % 2) * 2048 + 2048]
            if it >= 2:
                kb.wait("vector", "sT", evG[it - 2])
            ln2ct[it] = ln_tile(XT[it % 2], xh, 0)
            warmT(ln2ct[it])
            kb.wait("tensor", "sV", ln2ct[it])
            for half in range(2):
                src = xh[:, half * 1024:half * 1024 + 1024]
                dst = lambda n, half=half, it=it: (
                    A(BIG1, half * 8 * 1024 + it * 128, [[16384, 128], [1024, 8], [1, 128]]),
                    PT[half][:, 0:1024])
                tp_group(src, 8, half, dst)
            evG[it] = kb.cnt["sT"]
            vG[it] = kb.cnt["sV"]

        cut(10)
        def fuse_ap(c, qb):
            if c < 32:
                return BIG0[:, c * 1024 + qb * 512:c * 1024 + qb * 512 + 512]
            return BIG2[:, (c - 32) * 1024 + qb * 512:(c - 32) * 1024 + qb * 512 + 512]

        fuse_ct = {}
        gu_mm = {}
        sup_by = {}
        last_fu = [None]
        for io in range(48):
            bufp = io % 2
            if io >= 2:
                kb.wait("gpsimd", "sT", gu_mm[io - 2])
            gin = A(gatew, io * 128, [[6144, 128], [128 * 6144, 16], [1, 128]])
            uin = A(upw, io * 128, [[6144, 128], [128 * 6144, 16], [1, 128]])
            g1 = kb.dma("gpsimd", lambda e, gin=gin, bufp=bufp: e.dma_start(
                out=W8[:, bufp * 2048:bufp * 2048 + 2048], in_=gin))
            g2 = kb.dma("gpsimd", lambda e, uin=uin, bufp=bufp: e.dma_start(
                out=W8[:, 4096 + bufp * 2048:4096 + bufp * 2048 + 2048], in_=uin))
            kb.wait("tensor", *g1)
            kb.wait("tensor", *g2)
            for qb in range(2):
                gi = io * 2 + qb
                pb = 2 * (gi % 2)
                if io == 0:
                    kb.wait("tensor", "sV", vG[3] if qb == 0 else vG[7])
                prev = sup_by.get(gi - 2)
                if prev:
                    kb.wait("tensor", "sS", prev)
                for c in range(16):
                    mmg = kb.op("tensor", lambda e, c=c, qb=qb, bufp=bufp, pb=pb: e.matmul(
                        PS[pb][:, :], W8[:, bufp * 2048 + c * 128:bufp * 2048 + (c + 1) * 128],
                        BIG1[:, c * 1024 + qb * 512:c * 1024 + qb * 512 + 512],
                        start=(c == 0), stop=(c == 15)), inc="sT" if c == 15 else None)
                for c in range(16):
                    mmu = kb.op("tensor", lambda e, c=c, qb=qb, bufp=bufp, pb=pb: e.matmul(
                        PS[pb + 1][:, :],
                        W8[:, 4096 + bufp * 2048 + c * 128:4096 + bufp * 2048 + (c + 1) * 128],
                        BIG1[:, c * 1024 + qb * 512:c * 1024 + qb * 512 + 512],
                        start=(c == 0), stop=(c == 15)), inc="sT" if c == 15 else None)
                gu_mm[io] = mmu
                kb.wait("scalar", "sT", mmg)
                if last_fu[0]:
                    kb.wait("scalar", "sV", last_fu[0])
                s_g = kb.op("scalar", lambda e, io=io, pb=pb: e.activation(
                    PB[:, 0:512], PS[pb][:, :], AFT.Silu, scale=GSCc[:, io:io + 1]), inc="sS")
                kb.wait("scalar", "sT", mmu)
                s_up = kb.op("scalar", lambda e, io=io, pb=pb: e.activation(
                    PB[:, 512:1024], PS[pb + 1][:, :], AFT.Identity, scale=USCc[:, io:io + 1]), inc="sS")
                sup_by[gi] = s_up
                kb.wait("vector", "sS", s_up)
                fuse_ct[(io, qb)] = kb.op("vector", lambda e, io=io, qb=qb: e.tensor_mul(
                    fuse_ap(io, qb), PB[:, 0:512], PB[:, 512:1024]), inc="sV")
                last_fu[0] = fuse_ct[(io, qb)]
        cut(11)
        dn_mm3 = {}
        dn_ev = {}
        dn_tct = {}
        dn_st = {}
        dn_vdn = {}
        for dt in range(16):
            lds = {}
            for t3 in range(3):
                L = dt * 3 + t3
                slot = L % 4
                qd = "gpsimd"
                if L >= 4:
                    dtp, t3p = divmod(L - 4, 3)
                    kb.wait(qd, "sT", dn_mm3[(dtp, 1, t3p)])
                else:
                    kb.wait(qd, "sT", gu_mm[46 + (L % 2)])
                din_ap = A(downw, (t3 * 16) * 128 * 2048 + dt * 128,
                           [[2048, 128], [128 * 2048, 16], [1, 128]])
                lds[t3] = (slot, kb.dma(qd, lambda e, din_ap=din_ap, slot=slot: e.dma_start(
                    out=W8[:, slot * 2048:slot * 2048 + 2048], in_=din_ap)))
            for qb in range(2):
                gi = dt * 2 + qb
                pi = gi % 2
                pst = PS[4 + gi % 2]
                ate = F32A[:, 2048 + (gi % 2) * 512:2048 + (gi % 2) * 512 + 512]
                rslot = F32A[:, 3072:3584] if gi % 2 == 0 else F32A[:, 0:512]
                aout = F32A[:, 3584:4096] if gi % 2 == 0 else F32A[:, 1536:2048]
                prev = dn_vdn.get(gi - 2)
                if prev:
                    kb.wait("tensor", "sS", prev)
                if dt == 0:
                    kb.wait("tensor", "sV", fuse_ct[(47, qb)])
                    kb.wait("tensor", "sS", sup_by[95])
                for t3 in range(3):
                    slot, ld3 = lds[t3]
                    kb.wait("tensor", *ld3)
                    for cc in range(16):
                        c = t3 * 16 + cc
                        dn_mm3[(dt, qb, t3)] = kb.op("tensor", lambda e, slot=slot, cc=cc, c=c, qb=qb,
                                                     pi=pi, isf=(t3 == 0 and cc == 0),
                                                     isl=(t3 == 2 and cc == 15): e.matmul(
                            PS[pi][:, :], W8[:, slot * 2048 + cc * 128:slot * 2048 + (cc + 1) * 128],
                            fuse_ap(c, qb), start=isf, stop=isl),
                            inc="sT" if cc == 15 else None)
                mmd = dn_mm3[(dt, qb, 2)]
                kb.wait("scalar", "sT", mmd)
                prev_tct = dn_tct.get(gi - 2)
                if prev_tct:
                    kb.wait("scalar", "sT", prev_tct)
                v_dn = dn_vdn[gi] = kb.op("scalar", lambda e, pi=pi, dt=dt, ate=ate: e.activation(
                    ate, PS[pi][:, :], AFT.Identity, scale=DSCc[:, dt:dt + 1]), inc="sS")
                rin = A(res_d, (qb * 512) * 2048 + dt * 128, [[2048, 128], [128 * 2048, 4], [1, 128]])
                if gi < 2:
                    kb.wait("sync", "sV", ln2ct[7])
                prev_ev = dn_ev.get(gi - 2)
                if prev_ev:
                    kb.wait("sync", "sV", prev_ev)
                ld_blk = kb.dma("sync", lambda e, rin=rin, rslot=rslot: e.dma_start(
                    out=rslot, in_=rin))
                kb.wait("tensor", "sS", v_dn)
                if prev_ev:
                    kb.wait("tensor", "sV", prev_ev)
                for tq in range(4):
                    tct = kb.op("tensor", lambda e, tq=tq, ate=ate, pst=pst: e.matmul(
                        pst[:, tq * 128:(tq + 1) * 128], ate[:, tq * 128:(tq + 1) * 128], IDF[:, :],
                        is_transpose=True, start=(tq == 0), stop=(tq == 3)),
                        inc="sT" if tq == 3 else None)
                dn_tct[gi] = tct
                kb.wait("vector", "sT", tct)
                kb.wait("vector", *ld_blk)
                if (gi - 2) in dn_st:
                    kb.wait("vector", *dn_st[gi - 2])
                elif prev_store[0] and gi < 2:
                    kb.wait("vector", *prev_store[0])
                dn_ev[gi] = kb.op("vector", lambda e, pst=pst, aout=aout, rslot=rslot: e.tensor_add(
                    aout, pst[:, 0:512], rslot), inc="sV")
                kb.wait("sync", "sV", dn_ev[gi])
                oout = A(out_d, (qb * 512) * 2048 + dt * 128, [[2048, 128], [128 * 2048, 4], [1, 128]])
                dn_st[gi] = kb.dma("sync", lambda e, oout=oout, aout=aout: e.dma_start(
                    out=oout, in_=aout))

    except _Cut:
        pass
    kb.wait_all_dma("sync")
    kb.wait_all_dma("gpsimd")

    blk = es.enter_context(nc.Block())
    for name in ENGINES:
        def body(eng, name=name):
            for fn in kb.ops[name]:
                fn(eng)
        getattr(blk, name)(body)
    es.close()
    return nc


_NC = [None]


def kernel(L0_attn_x, L0_attn_freqs, L0_attn_wq_w, L0_attn_wq_b, L0_attn_wkv_a_w,
           L0_attn_wkv_b_w, L0_attn_norm_w, L0_attn_out_w, L0_attn_out_b,
           L0_ffn_gate_w, L0_ffn_up_w, L0_ffn_down_w, L0_ffn_gate_scale,
           L0_ffn_up_scale, L0_ffn_down_scale):
    f32 = lambda a: np.ascontiguousarray(np.asarray(a, dtype=np.float32))
    x = f32(L0_attn_x)
    cols = lambda v, n: np.ascontiguousarray(f32(v).reshape(n, 128).T)
    tr = lambda a: np.ascontiguousarray(f32(a).T)
    shared = {
        "wqT": tr(L0_attn_wq_w), "wkvaT": tr(L0_attn_wkv_a_w), "wkvbT": tr(L0_attn_wkv_b_w),
        "outwT": tr(L0_attn_out_w), "gatew": f32(L0_ffn_gate_w), "upw": f32(L0_ffn_up_w),
        "downw": f32(L0_ffn_down_w),
        "wqb_c": cols(L0_attn_wq_b, 16), "outb_c": cols(L0_attn_out_b, 16),
        "normw_c": cols(L0_attn_norm_w, 4), "gsc_c": cols(L0_ffn_gate_scale, 48),
        "usc_c": cols(L0_ffn_up_scale, 48), "dsc_c": cols(L0_ffn_down_scale, 16),
        "c2": np.ascontiguousarray(f32(L0_attn_freqs)[:16].reshape(16, 64).T),
        "m2": _m2_np(), "identb": np.eye(128).astype(bfd),
        "identf": np.eye(128, dtype=np.float32),
        "onesc": np.ones((128, 1), dtype=bfd),
        "onesr": np.ones((1, 128), dtype=np.float32),
        "masks": _masks_np(),
    }
    in_maps = []
    for c in range(8):
        b, h = c // 2, c % 2
        xbv = x[b] if h == 0 else np.ascontiguousarray(
            np.concatenate([x[b, 1024:], x[b, :1024]], axis=0))
        m = dict(shared)
        m["xb"] = xbv
        m["mbias"] = _mbias_np(h)
        in_maps.append(m)
    if _NC[0] is None:
        _NC[0] = build_nc()
    res = run_bass_kernel_spmd(_NC[0], in_maps, core_ids=list(range(8)))
    y = np.empty((4, 2048, 2048), dtype=np.float32)
    for c in range(8):
        b, h = c // 2, c % 2
        y[b, h * 1024:(h + 1) * 1024] = res.results[c]["out"]
    return y


def _m2_np():
    m2 = np.zeros((64, 128), dtype=np.float32)
    for i in range(32):
        for cc in range(2):
            for d in range(2):
                m2[2 * i + cc, 64 + 2 * i + d] = 1.0
    return m2


def _masks_np():
    i = np.arange(128)[:, None]
    j = np.arange(512)[None, :]
    ms = [(i + m * 128 <= j).astype(np.float32) for m in range(4)]
    return np.ascontiguousarray(np.concatenate(ms, axis=1).astype(bfd))


def _mbias_np(h):
    mb = np.zeros((2, 16), dtype=np.float32)
    if h == 0:
        mb[0, 4:16] = NEG
        mb[1, 8:16] = NEG
    flat = mb.reshape(32)
    return np.ascontiguousarray(np.repeat(flat[None, :], 128, axis=0))
